# version 13
# baseline (speedup 1.0000x reference)
import math
from contextlib import ExitStack

import numpy as np
import ml_dtypes

import concourse.bass as bass
import concourse.mybir as mybir
from concourse.bass_utils import run_bass_kernel_spmd

F32 = mybir.dt.float32
BF16 = mybir.dt.bfloat16
AF = mybir.ActivationFunctionType
ALU = mybir.AluOpType
AX = mybir.AxisListType

P = 128
T = 2048
H = 2048
TC = T // P
KC = H // P
NH, NKV, DH = 16, 4, 128
I = 8192
IC = I // P
GRP = 512
NGRP = I // GRP
HALF = 1024
ROPE_THETA = 10000.0
SCALE = 1.0 / math.sqrt(DH)
EPS = 1e-5

ENGINES = ("sync", "gpsimd", "tensor", "scalar", "vector")


class Sem:
    def __init__(self, handle):
        self.h = handle
        self.n = 0

    def bump(self, k):
        self.n += k
        return self.n


class Prog:

    def __init__(self):
        self.streams = {e: [] for e in ENGINES}

    def wait(self, eng, sem, val):
        if val is None:
            return
        self.streams[eng].append(lambda e, s=sem.h, v=val: e.wait_ge(s, v))

    def run(self, eng, fn, inc=None, k=1):
        if inc is None:
            self.streams[eng].append(fn)
            return None
        v = inc.bump(k)
        self.streams[eng].append(lambda e, f=fn, s=inc.h, kk=k: f(e).then_inc(s, kk))
        return v

    def replay(self, block):
        for name in ENGINES:
            stream = self.streams[name]

            def make(st):
                def body(eng):
                    for fn in st:
                        fn(eng)
                return body
            getattr(block, name)(make(stream))


def build_core_program(dumps=()):
    dumps = set(dumps)
    nc = bass.Bass("TRN2", target_bir_lowering=False, detect_race_conditions=False)

    x_in = nc.dram_tensor("x", [T, H], F32, kind="ExternalInput").ap()
    wq_r = nc.dram_tensor("wq_r", [NH, P, KC, DH], F32, kind="ExternalInput").ap()
    wk_r = nc.dram_tensor("wk_r", [NKV, P, KC, DH], F32, kind="ExternalInput").ap()
    wv_in = nc.dram_tensor("wv", [H, NKV * DH], F32, kind="ExternalInput").ap()
    wo_in = nc.dram_tensor("wo", [H, H], F32, kind="ExternalInput").ap()
    wg_r = nc.dram_tensor("wg_r", [NGRP, P, KC, GRP], F32, kind="ExternalInput").ap()
    wu_r = nc.dram_tensor("wu_r", [NGRP, P, KC, GRP], F32, kind="ExternalInput").ap()
    wd_r = nc.dram_tensor("wd_r", [KC, P, IC, P], F32, kind="ExternalInput").ap()
    ctab_in = nc.dram_tensor("ctab", [P, T], BF16, kind="ExternalInput").ap()
    stab_in = nc.dram_tensor("stab", [P, T], BF16, kind="ExternalInput").ap()
    perm_in = nc.dram_tensor("perm", [P, P], BF16, kind="ExternalInput").ap()
    ones_in = nc.dram_tensor("onesm", [P, P], BF16, kind="ExternalInput").ap()
    ident_in = nc.dram_tensor("ident", [P, P], BF16, kind="ExternalInput").ap()
    out_hbm = nc.dram_tensor("out", [T, H], F32, kind="ExternalOutput").ap()
    scr_r1 = nc.dram_tensor("scr_r1", [T, H], BF16, kind="Internal").ap()
    scr_x1t = nc.dram_tensor("scr_x1t", [P, KC * T], BF16, kind="Internal").ap()

    dump_aps = {}
    _dump_specs = {
        "xT": ([P, KC * T], BF16), "q0": ([P, T], BF16), "k0": ([P, T], BF16),
        "vall": ([P, TC * 512], BF16), "oallt": ([P, NH * T], BF16),
        "z0": ([P, 1024], F32), "x1bf": ([P, TC * H], BF16),
        "x1t": ([P, KC * T], BF16), "hst0": ([P, HALF], BF16), "dtm0": ([P, H], BF16),
        "gs0": ([P, GRP], BF16), "hb0": ([P, GRP], BF16),
    }
    for tag in dumps:
        shp, dt = _dump_specs[tag]
        dump_aps[tag] = nc.dram_tensor("dbg_" + tag, shp, dt, kind="ExternalOutput").ap()

    p = Prog()

    with ExitStack() as top:
        def sem(name):
            return Sem(top.enter_context(nc.semaphore(name)))

        T1 = top.enter_context(nc.sbuf_tensor("T1", [P, 32768], BF16))
        T2 = top.enter_context(nc.sbuf_tensor("T2", [P, 32768], BF16))
        T3 = top.enter_context(nc.sbuf_tensor("T3", [P, 32768], BF16))

        perm_sb = top.enter_context(nc.sbuf_tensor("perm_sb", [P, P], BF16))
        ones_sb = top.enter_context(nc.sbuf_tensor("ones_sb", [P, P], BF16))
        ident_sb = top.enter_context(nc.sbuf_tensor("ident_sb", [P, P], BF16))
        mcols = top.enter_context(nc.sbuf_tensor("mcols", [P, TC], F32))
        rcols = top.enter_context(nc.sbuf_tensor("rcols", [P, TC], F32))
        lncol = top.enter_context(nc.sbuf_tensor("lncol", [P, 16], F32))
        SCR = top.enter_context(nc.sbuf_tensor("SCR", [P, 7552], BF16))

        sQ = sem("sQ")
        sG = sem("sG")
        sPE = sem("sPE")
        sACT = sem("sACT")
        sDVE = sem("sDVE")
        sVS = sem("sVS")
        sSS = sem("sSS")
        sWV = sem("sWV")
        sXC = [sem(f"sXC{i}") for i in range(4)]
        sWB = [sem(f"sWB{i}") for i in range(2)]
        sXL = [sem(f"sXL{i}") for i in range(8)]
        sSP = sem("sSP")
        sWGp = [sem(f"sWG{i}") for i in range(2)]
        sWUp = [sem(f"sWU{i}") for i in range(2)]
        sWDp = [sem(f"sWD{i}") for i in range(2)]
        sHTg = [sem(f"sHTg{i}") for i in range(NGRP)]
        sPC = sem("sPC")
        sXSP = sem("sXSP")
        sXL2 = sem("sXTL")
        sRL = [sem(f"sRL{i}") for i in range(2)]

        def t_slot(tens, slot):
            return tens[:, slot * T:(slot + 1) * T]

        K_OFF, V_OFF, QSB_OFF = 0, 8192, 16384
        T1R_OFF, T2R_OFF, WDB_OFF, CT_OFF, ST_OFF = 18432, 20480, 22528, 26624, 28672
        EXP_OFF = QSB_OFF

        def kslot(g):
            return T3[:, K_OFF + g * T: K_OFF + (g + 1) * T]

        def vslot(tkc):
            return T3[:, V_OFF + tkc * 512: V_OFF + (tkc + 1) * 512]

        qsb_ap = T3[:, QSB_OFF:QSB_OFF + T]
        t1_ap = T3[:, T1R_OFF:T1R_OFF + T]
        t2_ap = T3[:, T2R_OFF:T2R_OFF + T]
        ctab_sb = T3[:, CT_OFF:CT_OFF + T]
        stab_sb = T3[:, ST_OFF:ST_OFF + T]

        def wdbslot(i):
            o = WDB_OFF + (i % 2) * 2048
            return T3[:, o:o + 2048]

        def expslot(i):
            o = EXP_OFF + (i % 3) * 1024
            return T3[:, o:o + 1024]

        X1T_OFF, WG_OFF, WU_OFF = 0, 16384, 24576

        def x1t_ap(kc, lo, n):
            o = X1T_OFF + kc * HALF + lo
            return T3[:, o:o + n]

        def wg_ap(kc):
            return T3[:, WG_OFF + kc * GRP: WG_OFF + (kc + 1) * GRP]

        def wu_ap(kc):
            return T3[:, WU_OFF + kc * GRP: WU_OFF + (kc + 1) * GRP]

        def wd_slot(i):
            o = WG_OFF + (i % 2) * 8192
            return T3[:, o:o + 8192]

        def hsT_ap(ic, lo=0, n=HALF):
            tens = T1 if ic < 32 else T2
            o = (ic % 32) * HALF + lo
            return tens[:, o:o + n]

        def downT_ap(hc, lo=0, n=HALF):
            return T3[:, hc * HALF + lo: hc * HALF + lo + n]

        def dtm_ap(tql, lo=0, n=H):
            return T1[:, tql * H + lo: tql * H + lo + n]

        r1l2 = [T2[:, 0:H], T2[:, H:2 * H]]
        r2l2 = [T2[:, 2 * H:3 * H], T2[:, 3 * H:4 * H]]

        xin_v = x_in.rearrange("(tc p) h -> p tc h", p=P)
        xbf_v = T2[:].rearrange("p (tc h) -> p tc h", tc=TC)
        wv_v = wv_in.rearrange("(kc p) c -> p kc c", p=P)
        wvsb_v = T3[:, 0:8192].rearrange("p (kc c) -> p kc c", kc=KC)
        wv_done = p.run("gpsimd", lambda e: e.dma_start(out=wvsb_v, in_=wv_v), inc=sWV, k=16)

        xcast_done = [None] * 4
        for i in range(4):
            xcast_done[i] = p.run("gpsimd", lambda e, i=i: e.dma_start(
                out=xbf_v[:, i * 4:(i + 1) * 4, :], in_=xin_v[:, i * 4:(i + 1) * 4, :]),
                inc=sXC[i], k=16)

        p.run("sync", lambda e: e.dma_start(out=ctab_sb, in_=ctab_in), inc=sQ, k=16)
        p.run("sync", lambda e: e.dma_start(out=stab_sb, in_=stab_in), inc=sQ, k=16)
        p.run("sync", lambda e: e.dma_start(out=perm_sb[:], in_=perm_in), inc=sQ, k=16)
        p.run("sync", lambda e: e.dma_start(out=ones_sb[:], in_=ones_in), inc=sQ, k=16)
        tabs_done = p.run("sync", lambda e: e.dma_start(out=ident_sb[:], in_=ident_in), inc=sQ, k=16)

        T1_hview = T1[:].rearrange("p (hc t) -> p hc t", hc=KC)
        a_psum = ExitStack()
        a_ps = [a_psum.enter_context(nc.psum_tensor(f"a_ps{i}", [P, H], BF16)) for i in range(2)]
        xT_done = []
        prev_evc = [None, None]
        p.wait("tensor", sQ, tabs_done)
        for tc in range(TC):
            aps = a_ps[tc % 2]
            p.wait("tensor", sXC[tc // 4], xcast_done[tc // 4])
            p.wait("tensor", sACT, prev_evc[tc % 2])
            for hc in range(KC):
                p.run("tensor", lambda e, tc=tc, hc=hc, aps=aps: e.transpose(
                    aps[:, hc * P:(hc + 1) * P],
                    T2[:, tc * H + hc * P: tc * H + (hc + 1) * P], ident_sb[:]),
                    inc=sPE if hc == KC - 1 else None)
            trd = sPE.n
            p.wait("scalar", sPE, trd)
            v = p.run("scalar", lambda e, tc=tc, aps=aps: e.copy(
                T1_hview[:, :, tc * P:(tc + 1) * P], aps[:]), inc=sACT)
            prev_evc[tc % 2] = v
            xT_done.append(v)

        if "xT" in dumps:
            p.wait("sync", sACT, xT_done[-1])
            p.run("sync", lambda e: e.dma_start(out=dump_aps["xT"], in_=T1[:]), inc=sQ, k=16)

        a_psum.close()
        b_psum = ExitStack()
        ps_bq = b_psum.enter_context(nc.psum_tensor("ps_bq", [P, T], F32))
        ps_bp = b_psum.enter_context(nc.psum_tensor("ps_bp", [P, T], F32))

        p.wait("tensor", sACT, xT_done[-1])
        p.wait("tensor", sWV, wv_done)
        v_evac = [None] * TC
        for tkc in range(TC):
            sl = ps_bq[:, (tkc % 4) * 512:(tkc % 4) * 512 + 512]
            if tkc >= 4:
                p.wait("tensor", sACT, v_evac[tkc - 4])
            for kc in range(KC):
                p.run("tensor", lambda e, kc=kc, tkc=tkc, sl=sl: e.matmul(
                    sl, T1[:, kc * T + tkc * P: kc * T + (tkc + 1) * P],
                    wvsb_v[:, kc, :], start=(kc == 0), stop=(kc == KC - 1)),
                    inc=sPE if kc == KC - 1 else None)
            mmv = sPE.n
            p.wait("scalar", sPE, mmv)
            v_evac[tkc] = p.run("scalar", lambda e, tkc=tkc, sl=sl: e.copy(vslot(tkc), sl), inc=sACT)

        if "vall" in dumps:
            p.wait("sync", sACT, v_evac[-1])
            p.run("sync", lambda e: e.dma_start(
                out=dump_aps["vall"], in_=T3[:, V_OFF:V_OFF + TC * 512]), inc=sQ, k=16)

        head_rope_done = [None] * NH
        kv_rope_done = [None] * NKV
        trk = {
            "prev_wmm": [None, None],
            "prev_evac": v_evac[TC - 1],
            "prev_t2": None,
        }

        def emit_proj_rope(idx, wr_dram, dst_ap, done_list):
            wsl = wdbslot(idx)
            wsb_v = wsl.rearrange("p (kc c) -> p kc c", kc=KC)
            win_v = wr_dram[idx]
            if trk["prev_wmm"][idx % 2] is not None:
                p.wait("gpsimd", sPE, trk["prev_wmm"][idx % 2])
            wdone = p.run("gpsimd", lambda e, wsb_v=wsb_v, win_v=win_v: e.dma_start(
                out=wsb_v, in_=win_v), inc=sWB[idx % 2], k=16)

            p.wait("tensor", sWB[idx % 2], wdone)
            p.wait("tensor", sACT, trk["prev_evac"])
            for kc in range(KC):
                for n in range(4):
                    p.run("tensor", lambda e, kc=kc, n=n, wsb_v=wsb_v: e.matmul(
                        ps_bq[:, n * 512:(n + 1) * 512], wsb_v[:, kc, :],
                        T1[:, kc * T + n * 512: kc * T + (n + 1) * 512],
                        start=(kc == 0), stop=(kc == KC - 1)),
                        inc=sPE if (kc == KC - 1 and n == 3) else None)
            mm_done = sPE.n
            trk["prev_wmm"][idx % 2] = mm_done

            p.wait("scalar", sPE, mm_done)
            p.wait("scalar", sVS, trk["prev_t2"])
            evac = p.run("scalar", lambda e: e.copy(qsb_ap, ps_bq[:]), inc=sACT)
            trk["prev_evac"] = evac

            p.wait("tensor", sACT, evac)
            p.wait("tensor", sVS, trk["prev_t2"])
            for n in range(4):
                p.run("tensor", lambda e, n=n: e.matmul(
                    ps_bp[:, n * 512:(n + 1) * 512], perm_sb[:],
                    qsb_ap[:, n * 512:(n + 1) * 512], start=True, stop=True),
                    inc=sPE if n == 3 else None)
            pm_done = sPE.n

            p.wait("vector", sACT, evac)
            p.run("vector", lambda e: e.tensor_mul(t1_ap, qsb_ap, ctab_sb))
            p.wait("vector", sPE, pm_done)
            t2v = p.run("vector", lambda e: e.tensor_mul(t2_ap, ps_bp[:], stab_sb), inc=sVS)
            trk["prev_t2"] = t2v
            p.wait("vector", sVS, t2v)
            done_list[idx] = p.run("vector", lambda e, d=dst_ap: e.tensor_add(d, t1_ap, t2_ap), inc=sDVE)

        p.wait("tensor", sQ, tabs_done)
        p.wait("vector", sQ, tabs_done)
        for g in range(NKV):
            emit_proj_rope(g, wk_r, kslot(g), kv_rope_done)
        for h in range(NH):
            emit_proj_rope(h, wq_r, t_slot(T2, h), head_rope_done)

        if "q0" in dumps:
            p.wait("sync", sDVE, head_rope_done[0])
            p.run("sync", lambda e: e.dma_start(out=dump_aps["q0"], in_=t_slot(T2, 0)), inc=sQ, k=16)
        if "k0" in dumps:
            p.wait("sync", sDVE, kv_rope_done[0])
            p.run("sync", lambda e: e.dma_start(out=dump_aps["k0"], in_=kslot(0)), inc=sQ, k=16)

        b_psum.close()

        c_psum = ExitStack()
        ps_sc = c_psum.enter_context(nc.psum_tensor("ps_sc", [P, 3072], F32))
        ps_pv = c_psum.enter_context(nc.psum_tensor("ps_pv", [P, 1024], F32))
        zk = [SCR[:, i * 1024:(i + 1) * 1024] for i in range(2)]
        zrec = SCR[:, 2048:4096].bitcast(F32)

        p.wait("tensor", sVS, trk["prev_t2"])
        p.wait("scalar", sVS, trk["prev_t2"])
        p.wait("tensor", sDVE, kv_rope_done[NKV - 1])

        head_done = [None] * NH
        last_exp3 = [None, None, None]
        last_z3 = [None, None, None]
        prev_pv_evac = None
        prev_zrec = None
        for h in range(NH):
            g = h // NKV
            p.wait("tensor", sDVE, head_rope_done[h])
            for slab in range(2):
                qslab = t_slot(T2, h)[:, slab * 1024:(slab + 1) * 1024]
                exp_of = {}
                zadd_of = {}

                def scores(c):
                    third = c % 3
                    p.wait("tensor", sACT, last_exp3[third])
                    if third == 2:
                        p.wait("tensor", sSS, prev_zrec)
                    for n in range(2):
                        p.run("tensor", lambda e, c=c, n=n, third=third, qslab=qslab, g=g: e.matmul(
                            ps_sc[:, third * 1024 + n * 512: third * 1024 + (n + 1) * 512],
                            kslot(g)[:, c * P:(c + 1) * P],
                            qslab[:, n * 512:(n + 1) * 512], start=True, stop=True),
                            inc=sPE if n == 1 else None)
                    sc_done = sPE.n
                    p.wait("scalar", sPE, sc_done)
                    p.wait("scalar", sVS, last_z3[third])
                    ev = p.run("scalar", lambda e, c=c, third=third: e.activation(
                        expslot(c), ps_sc[:, third * 1024:(third + 1) * 1024],
                        AF.Exp, scale=SCALE), inc=sACT)
                    last_exp3[third] = ev
                    exp_of[c] = ev
                    p.wait("vector", sACT, ev)
                    if c == 0:
                        zv = p.run("vector", lambda e, c=c: e.tensor_copy(
                            zk[0][:], expslot(c)), inc=sVS)
                    else:
                        p.wait("vector", sVS, zadd_of[c - 1])
                        zv = p.run("vector", lambda e, c=c: e.tensor_add(
                            zk[c % 2][:], zk[(c - 1) % 2][:], expslot(c)), inc=sVS)
                    last_z3[third] = zv
                    zadd_of[c] = zv

                def pv(c):
                    p.wait("tensor", sACT, exp_of[c])
                    if c == 0 and prev_pv_evac is not None:
                        p.wait("tensor", sDVE, prev_pv_evac)
                    for n in range(2):
                        p.run("tensor", lambda e, c=c, n=n, g=g: e.matmul(
                            ps_pv[:, n * 512:(n + 1) * 512],
                            vslot(c)[:, g * P:(g + 1) * P],
                            expslot(c)[:, n * 512:(n + 1) * 512],
                            start=(c == 0), stop=(c == TC - 1)),
                            inc=sPE if (n == 1 and c == TC - 1) else None)

                scores(0)
                scores(1)
                scores(2)
                for c in range(3, TC):
                    pv(c - 3)
                    scores(c)
                pv(TC - 3)
                pv(TC - 2)
                p.wait("tensor", sVS, zadd_of[TC - 1])
                for n in range(2):
                    p.run("tensor", lambda e, n=n: e.matmul(
                        ps_sc[:, 2048 + n * 512: 2048 + (n + 1) * 512], ones_sb[:],
                        zk[(TC - 1) % 2][:, n * 512:(n + 1) * 512],
                        start=True, stop=True), inc=sPE if n == 1 else None)
                zmm_done = sPE.n
                pv(TC - 1)
                pv_done = sPE.n

                p.wait("scalar", sPE, zmm_done)
                p.wait("scalar", sDVE, prev_pv_evac)
                lnz = p.run("scalar", lambda e: e.activation(
                    zrec[:], ps_sc[:, 2048:3072], AF.Ln), inc=sSS)
                p.wait("scalar", sSS, lnz)
                zrv = p.run("scalar", lambda e: e.activation(
                    zrec[:], zrec[:], AF.Exp, scale=-1.0), inc=sACT)
                p.wait("vector", sPE, pv_done)
                p.wait("vector", sACT, zrv)
                dst = t_slot(T1, h)[:, slab * 1024:(slab + 1) * 1024]
                prev_pv_evac = p.run("vector", lambda e, dst=dst: e.tensor_mul(
                    dst, ps_pv[:], zrec[:]), inc=sDVE)
                prev_zrec = lnz
                if "z0" in dumps and h == 0 and slab == 0:
                    p.wait("sync", sACT, zrv)
                    p.run("sync", lambda e: e.dma_start(
                        out=dump_aps["z0"], in_=zrec[:]), inc=sQ, k=16)
            head_done[h] = prev_pv_evac

            wo_v = wo_in.rearrange("(kc p) c -> p kc c", p=P)[:, h, :]
            p.wait("gpsimd", sDVE, head_done[h])
            p.run("gpsimd", lambda e, h=h, wo_v=wo_v: e.dma_start(
                out=t_slot(T2, h), in_=wo_v), inc=sG, k=16)
        wo_loaded = sG.n

        if "oallt" in dumps:
            p.wait("sync", sDVE, head_done[-1])
            p.run("sync", lambda e: e.dma_start(out=dump_aps["oallt"], in_=T1[:]), inc=sQ, k=16)

        c_psum.close()

        d_psum = ExitStack()
        ps_wo = [d_psum.enter_context(nc.psum_tensor(f"ps_wo{i}", [P, H], F32)) for i in range(2)]
        r1_sb = SCR[:, 0:4096].bitcast(F32)

        scr_rows = scr_r1.rearrange("(tc p) h -> tc p h", p=P)
        out_rows = out_hbm.rearrange("(tc p) h -> tc p h", p=P)
        xin_rows = x_in.rearrange("(tc p) h -> tc p h", p=P)

        xload_done = [None] * TC
        for tqc in range(8):
            xload_done[tqc] = p.run("gpsimd", lambda e, tqc=tqc: e.dma_start(
                out=t_slot(T3, tqc), in_=xin_rows[tqc]), inc=sXL[tqc % 8], k=16)

        r1_done = [None] * TC
        r1_spill = [None] * TC
        sq_done = [None] * TC
        x1bf_done = [None] * TC
        womm_done = [None] * TC
        p.wait("tensor", sG, wo_loaded)
        for tqc in range(TC):
            ps = ps_wo[tqc % 2]
            if tqc >= 2:
                p.wait("tensor", sDVE, r1_done[tqc - 2])
            for kc in range(KC):
                for n in range(4):
                    p.run("tensor", lambda e, kc=kc, n=n, tqc=tqc, ps=ps: e.matmul(
                        ps[:, n * 512:(n + 1) * 512],
                        t_slot(T1, kc)[:, tqc * P:(tqc + 1) * P],
                        t_slot(T2, kc)[:, n * 512:(n + 1) * 512],
                        start=(kc == 0), stop=(kc == KC - 1)),
                        inc=sPE if (kc == KC - 1 and n == 3) else None)
            wo_done = sPE.n
            womm_done[tqc] = wo_done

            p.wait("vector", sXL[tqc % 8], xload_done[tqc])
            p.wait("vector", sPE, wo_done)
            if tqc >= 1:
                p.wait("vector", sSP, r1_spill[tqc - 1])
                p.wait("vector", sACT, sq_done[tqc - 1])
            rd = p.run("vector", lambda e, ps=ps, tqc=tqc: e.tensor_add(
                r1_sb[:], t_slot(T3, tqc), ps[:]), inc=sDVE)
            r1_done[tqc] = rd

            p.wait("gpsimd", sDVE, rd)
            r1_spill[tqc] = p.run("gpsimd", lambda e, tqc=tqc: e.dma_start(
                out=scr_rows[tqc], in_=r1_sb[:]), inc=sSP, k=16)
            if tqc + 8 < TC:
                xload_done[tqc + 8] = p.run("gpsimd", lambda e, t=tqc + 8: e.dma_start(
                    out=t_slot(T3, t), in_=xin_rows[t]), inc=sXL[tqc % 8], k=16)

            p.wait("scalar", sDVE, rd)
            cp = p.run("scalar", lambda e, tqc=tqc: e.activation(
                t_slot(T3, tqc), r1_sb[:], AF.Copy, accum_out=lncol[:, 1:2]), inc=sACT)
            sq = p.run("scalar", lambda e, tqc=tqc: e.activation(
                t_slot(T3, tqc), r1_sb[:], AF.Square, accum_out=lncol[:, 0:1]), inc=sACT)
            sq_done[tqc] = sq
            p.wait("vector", sACT, cp)
            mv = p.run("vector", lambda e, tqc=tqc: e.tensor_scalar(
                mcols[:, tqc:tqc + 1], lncol[:, 1:2], 1.0 / H, None, ALU.mult), inc=sVS)
            p.wait("vector", sVS, mv)
            m2 = p.run("vector", lambda e, tqc=tqc: e.tensor_scalar(
                lncol[:, 2:3], mcols[:, tqc:tqc + 1], mcols[:, tqc:tqc + 1], -EPS,
                ALU.mult, ALU.add), inc=sVS)
            p.wait("vector", sACT, sq)
            p.wait("vector", sVS, m2)
            vpe = p.run("vector", lambda e: e.tensor_scalar(
                lncol[:, 3:4], lncol[:, 0:1], 1.0 / H, lncol[:, 2:3],
                ALU.mult, ALU.subtract), inc=sVS)
            p.wait("scalar", sVS, vpe)
            lnv = p.run("scalar", lambda e: e.activation(lncol[:, 4:5], lncol[:, 3:4], AF.Ln), inc=sSS)
            p.wait("scalar", sSS, lnv)
            rs = p.run("scalar", lambda e, tqc=tqc: e.activation(
                rcols[:, tqc:tqc + 1], lncol[:, 4:5], AF.Exp, scale=-0.5), inc=sACT)
            p.wait("vector", sACT, rs)
            x1bf_done[tqc] = p.run("vector", lambda e, tqc=tqc: e.tensor_scalar(
                t_slot(T3, tqc), r1_sb[:], mcols[:, tqc:tqc + 1], rcols[:, tqc:tqc + 1],
                ALU.subtract, ALU.mult), inc=sDVE)

        if "x1bf" in dumps:
            p.wait("sync", sDVE, x1bf_done[-1])
            p.run("sync", lambda e: e.dma_start(out=dump_aps["x1bf"], in_=T3[:]), inc=sQ, k=16)

        d_psum.close()

        T2_hview = T2[:].rearrange("p (kc t) -> p kc t", kc=KC)
        dx_stack = ExitStack()
        dx_pad = dx_stack.enter_context(nc.psum_tensor("dx_pad", [P, H], F32))
        dx_ps = [dx_stack.enter_context(nc.psum_tensor(f"dx_ps{i}", [P, H], BF16)) for i in range(2)]
        x1t_evac = [None] * TC
        p.wait("tensor", sDVE, r1_done[15])
        for tc in range(TC):
            dps = dx_ps[tc % 2]
            p.wait("tensor", sDVE, x1bf_done[tc])
            p.wait("tensor", sACT, x1t_evac[tc - 2] if tc >= 2 else None)
            for hc in range(KC):
                p.run("tensor", lambda e, tc=tc, hc=hc, dps=dps: e.transpose(
                    dps[:, hc * P:(hc + 1) * P],
                    T3[:, tc * H + hc * P: tc * H + (hc + 1) * P], ident_sb[:]),
                    inc=sPE if hc == KC - 1 else None)
            trd = sPE.n
            p.wait("scalar", sPE, trd)
            x1t_evac[tc] = p.run("scalar", lambda e, tc=tc, dps=dps: e.copy(
                T2_hview[:, :, tc * P:(tc + 1) * P], dps[:]), inc=sACT)
        wgfree = x1t_evac[11]
        wufree = x1t_evac[15]
        x1t_pieces = []
        T3x_v = T3[:, 0:KC * HALF].rearrange("p (kc t) -> p kc t", kc=KC)
        p.wait("sync", sACT, x1t_evac[7])
        for blk in range(2):
            v = p.run("sync", lambda e, blk=blk: e.dma_start(
                out=T3x_v[:, :, blk * 512:(blk + 1) * 512],
                in_=T2_hview[:, :, blk * 512:(blk + 1) * 512]), inc=sPC, k=16)
            x1t_pieces.append(v)
        p.wait("sync", sACT, x1t_evac[15])
        x1t_spill = p.run("sync", lambda e: e.dma_start(out=scr_x1t, in_=T2[:]), inc=sXSP, k=16)
        if "x1t" in dumps:
            p.run("sync", lambda e: e.dma_start(out=dump_aps["x1t"], in_=T2[:]), inc=sQ, k=16)
        dx_stack.close()

        e_psum = ExitStack()
        ps_g4 = [e_psum.enter_context(nc.psum_tensor(f"ps_g{i}", [P, GRP], F32)) for i in range(4)]
        ps_d = [e_psum.enter_context(nc.psum_tensor(f"ps_d{i}", [P, HALF], F32)) for i in range(2)]

        scr_x1t_v = scr_x1t.rearrange("p (kc t) -> p kc t", kc=KC)
        wgsb_v = T3[:, WG_OFF:WG_OFF + KC * GRP].rearrange("p (kc c) -> p kc c", kc=KC)
        wusb_v = T3[:, WU_OFF:WU_OFF + KC * GRP].rearrange("p (kc c) -> p kc c", kc=KC)

        prev_half_dmm = None
        prev_half_hb = None
        prev_half_hstr2 = [None, None]
        prev_half_lnsq = None
        ln2_last_v = None
        ln2_last_store = None
        next_x1t_load = None
        next_wg0 = None
        for half in range(2):
            x1t_load = None if half == 0 else next_x1t_load

            gsilu = SCR[:, 0:8 * GRP]
            gsilu_g0 = T2[:, 4096:8192]

            def gs_ap(grp):
                return gsilu_g0 if (half == 1 and grp == 0) else gsilu
            hbuf2 = [SCR[:, 4096 + i * GRP: 4096 + (i + 1) * GRP] for i in range(2)]
            hsq2 = [SCR[:, 5120 + i * GRP: 5120 + (i + 1) * GRP] for i in range(2)]
            mmx = SCR[:, 6656:6656 + 7 * (GRP // 4)]

            wg_done = [None] * NGRP
            wu_done = [None] * NGRP
            gmm_done = [None] * NGRP
            umm_done = [None] * NGRP
            silu_done = [[None] * 8 for _ in range(NGRP)]
            hb_done = [[None] * 8 for _ in range(NGRP)]
            hs_tr = [[None] * 8 for _ in range(NGRP)]
            sq_done2 = [[None] * 8 for _ in range(NGRP)]
            cmp_done2 = [[None] * 8 for _ in range(NGRP)]

            if half == 0:
                p.wait("gpsimd", sACT, wgfree)
                wl = p.run("gpsimd", lambda e: e.dma_start(
                    out=wgsb_v[:, 0:8, :], in_=wg_r[0][:, 0:8, :]), inc=sWGp[0], k=16)
                wh = p.run("gpsimd", lambda e: e.dma_start(
                    out=wgsb_v[:, 8:16, :], in_=wg_r[0][:, 8:16, :]), inc=sWGp[1], k=16)
                wg_done[0] = (wl, wh)
            else:
                wg_done[0] = next_wg0

            for grp in range(NGRP):
                if half == 0 and grp < 8:
                    xap = lambda kc, tq: T2[:, kc * T + tq * P: kc * T + tq * P + P]
                else:
                    xap = lambda kc, tq: x1t_ap(kc, tq * P, P)
                for tq in range(8):
                    if grp == 0:
                        if half == 0:
                            p.wait("tensor", sACT, x1t_evac[tq])
                            p.wait("tensor", sPC, x1t_pieces[-1])
                        elif tq == 0:
                            p.wait("tensor", sXL2, x1t_load)
                    if half == 0 and grp == 8 and tq == 0:
                        p.wait("tensor", sPC, x1t_pieces[-1])
                    ps = ps_g4[tq % 4]
                    if tq >= 4:
                        p.wait("tensor", sACT, silu_done[grp][tq - 4])
                    elif grp > 0:
                        p.wait("tensor", sDVE, hb_done[grp - 1][tq + 4])
                    elif half == 1:
                        p.wait("tensor", sDVE, prev_half_hb)
                    for kc in range(KC):
                        if tq == 0 and kc in (0, 8):
                            p.wait("tensor", sWGp[kc // 8], wg_done[grp][kc // 8])
                        p.run("tensor", lambda e, kc=kc, tq=tq, ps=ps: e.matmul(
                            ps[:], xap(kc, tq), wg_ap(kc),
                            start=(kc == 0), stop=(kc == KC - 1)),
                            inc=sPE if kc == KC - 1 else None)
                    gd = sPE.n
                    p.wait("scalar", sPE, gd)
                    if grp > 0:
                        p.wait("scalar", sDVE, hb_done[grp - 1][tq])
                        if half == 1 and grp == 1 and tq == 0:
                            p.wait("scalar", sQ, ln2_last_store)
                    elif half == 0:
                        p.wait("scalar", sDVE, x1bf_done[TC - 1])
                        p.wait("scalar", sSP, r1_spill[TC - 1])
                    silu_done[grp][tq] = p.run("scalar", lambda e, tq=tq, ps=ps, gs=gs_ap(grp): e.activation(
                        gs[:, tq * GRP:(tq + 1) * GRP], ps[:], AF.Silu), inc=sACT)
                    if "gs0" in dumps and half == 0 and grp == 0 and tq == 0:
                        p.wait("sync", sACT, silu_done[grp][tq])
                        p.run("sync", lambda e: e.dma_start(out=dump_aps["gs0"], in_=gsilu[:, 0:GRP]), inc=sQ, k=16)
                gmm_done[grp] = sPE.n
                if grp > 0:
                    p.wait("gpsimd", sPE, umm_done[grp - 1])
                elif half == 0:
                    p.wait("gpsimd", sACT, wufree)
                wul = p.run("gpsimd", lambda e, grp=grp: e.dma_start(
                    out=wusb_v[:, 0:8, :], in_=wu_r[grp][:, 0:8, :]), inc=sWUp[0], k=16)
                wuh = p.run("gpsimd", lambda e, grp=grp: e.dma_start(
                    out=wusb_v[:, 8:16, :], in_=wu_r[grp][:, 8:16, :]), inc=sWUp[1], k=16)
                wu_done[grp] = (wul, wuh)

                Q4 = GRP // 4
                sq_of = sq_done2[grp]
                cmp_of = cmp_done2[grp]

                def sparsify(tq):
                    s_buf = hsq2[tq % 2]
                    hb_buf = hbuf2[tq % 2]
                    p.wait("vector", sACT, sq_of[tq])
                    s2a = s_buf[:].rearrange("p (gh two) -> p gh two", two=2)
                    lob = mmx[:, 0:2 * Q4]
                    hib = mmx[:, 2 * Q4:4 * Q4]
                    mx = mmx[:, 4 * Q4:5 * Q4]
                    mn = mmx[:, 5 * Q4:6 * Q4]
                    tt = mmx[:, 6 * Q4:7 * Q4]
                    mask = mmx[:, 0:4 * Q4]
                    p.run("vector", lambda e, s2a=s2a, lob=lob: e.tensor_tensor(
                        lob, s2a[:, :, 0], s2a[:, :, 1], ALU.min))
                    v2 = p.run("vector", lambda e, s2a=s2a, hib=hib: e.tensor_tensor(
                        hib, s2a[:, :, 0], s2a[:, :, 1], ALU.max), inc=sVS)
                    p.wait("vector", sVS, v2)
                    lob2 = lob.rearrange("p (g two) -> p g two", two=2)
                    hib2 = hib.rearrange("p (g two) -> p g two", two=2)
                    p.run("vector", lambda e, lob2=lob2, mx=mx: e.tensor_tensor(
                        mx, lob2[:, :, 0], lob2[:, :, 1], ALU.max))
                    v3 = p.run("vector", lambda e, hib2=hib2, mn=mn: e.tensor_tensor(
                        mn, hib2[:, :, 0], hib2[:, :, 1], ALU.min), inc=sVS)
                    p.wait("vector", sVS, v3)
                    v4 = p.run("vector", lambda e, mx=mx, mn=mn, tt=tt: e.tensor_tensor(
                        tt, mx, mn, ALU.min), inc=sVS)
                    p.wait("vector", sVS, v4)
                    sview = s_buf[:].rearrange("p (g four) -> p g four", four=4)
                    mview = mask.rearrange("p (g four) -> p g four", four=4)
                    for j in range(4):
                        p.run("vector", lambda e, sview=sview, mview=mview, j=j, tt=tt: e.tensor_tensor(
                            mview[:, :, j], sview[:, :, j], tt, ALU.is_gt),
                            inc=sVS if j == 3 else None)
                    v8 = sVS.n
                    cmp_of[tq] = v8
                    p.wait("vector", sVS, v8)
                    ha = p.run("vector", lambda e, hb_buf=hb_buf, mask=mask: e.tensor_mul(
                        hb_buf[:], hb_buf[:], mask), inc=sDVE)

                    ic0 = grp * 4
                    tens = T1 if ic0 < 32 else T2
                    oview = tens[:].rearrange("p (ic t) -> p ic t", ic=32)[
                        :, (ic0 % 32):(ic0 % 32) + 4, tq * P:(tq + 1) * P]
                    if half == 1 and grp == 0 and tq == 0:
                        p.wait("sync", sDVE, ln2_last_v)
                    p.wait("sync", sDVE, ha)
                    hs_tr[grp][tq] = p.run("sync", lambda e, oview=oview, hb_buf=hb_buf: e.dma_start(
                        out=oview, in_=hb_buf[:], transpose=True), inc=sHTg[grp], k=16)

                for tq in range(8):
                    ps = ps_g4[tq % 4]
                    if tq >= 4:
                        p.wait("tensor", sDVE, hb_done[grp][tq - 4])
                    else:
                        p.wait("tensor", sACT, silu_done[grp][4 + tq])
                    for kc in range(KC):
                        if tq == 0 and kc in (0, 8):
                            p.wait("tensor", sWUp[kc // 8], wu_done[grp][kc // 8])
                        p.run("tensor", lambda e, kc=kc, tq=tq, ps=ps: e.matmul(
                            ps[:], xap(kc, tq), wu_ap(kc),
                            start=(kc == 0), stop=(kc == KC - 1)),
                            inc=sPE if kc == KC - 1 else None)
                    ud = sPE.n
                    p.wait("vector", sPE, ud)
                    if tq >= 2:
                        p.wait("vector", sACT, sq_of[tq - 2])
                        p.wait("vector", sHTg[grp], hs_tr[grp][tq - 2])
                    elif grp > 0:
                        p.wait("vector", sACT, sq_done2[grp - 1][tq + 6])
                        p.wait("vector", sHTg[grp - 1], hs_tr[grp - 1][tq + 6])
                    elif half == 1:
                        if tq == 0:
                            p.wait("vector", sACT, prev_half_lnsq)
                        p.wait("vector", sHTg[NGRP - 1], prev_half_hstr2[tq])
                    hb = p.run("vector", lambda e, tq=tq, ps=ps, gs=gs_ap(grp): e.tensor_mul(
                        hbuf2[tq % 2][:], gs[:, tq * GRP:(tq + 1) * GRP], ps[:]), inc=sDVE)
                    hb_done[grp][tq] = hb
                    p.wait("scalar", sDVE, hb)
                    if tq >= 2:
                        p.wait("scalar", sVS, cmp_of[tq - 2])
                    elif grp > 0:
                        p.wait("scalar", sVS, cmp_done2[grp - 1][tq + 6])
                    sq_of[tq] = p.run("scalar", lambda e, tq=tq: e.activation(
                        hsq2[tq % 2][:], hbuf2[tq % 2][:], AF.Square), inc=sACT)
                    if "hb0" in dumps and half == 0 and grp == 0 and tq == 0:
                        p.wait("sync", sDVE, hb)
                        p.run("sync", lambda e: e.dma_start(out=dump_aps["hb0"], in_=hbuf2[0][:]), inc=sQ, k=16)
                    if tq >= 1:
                        sparsify(tq - 1)
                sparsify(7)
                umm_done[grp] = sPE.n
                if grp + 1 < NGRP:
                    p.wait("gpsimd", sPE, gmm_done[grp])
                    wl = p.run("gpsimd", lambda e, grp=grp: e.dma_start(
                        out=wgsb_v[:, 0:8, :], in_=wg_r[grp + 1][:, 0:8, :]), inc=sWGp[0], k=16)
                    wh = p.run("gpsimd", lambda e, grp=grp: e.dma_start(
                        out=wgsb_v[:, 8:16, :], in_=wg_r[grp + 1][:, 8:16, :]), inc=sWGp[1], k=16)
                    wg_done[grp + 1] = (wl, wh)

            prev_half_hb = hb_done[NGRP - 1][7]
            prev_half_hstr2 = [hs_tr[NGRP - 1][6], hs_tr[NGRP - 1][7]]

            if "hst0" in dumps and half == 0:
                p.wait("sync", sHTg[NGRP - 1], hs_tr[NGRP - 1][7])
                p.run("sync", lambda e: e.dma_start(out=dump_aps["hst0"], in_=hsT_ap(0)), inc=sQ, k=16)


            wd_done = [None] * KC
            dmm_done = [None] * KC
            devac_done = [None] * KC
            p.wait("gpsimd", sPE, gmm_done[NGRP - 1])
            wd_done[0] = p.run("gpsimd", lambda e: e.dma_start(
                out=wd_slot(0).rearrange("p (ic c) -> p ic c", ic=IC),
                in_=wd_r[0]), inc=sWDp[0], k=16)
            p.wait("gpsimd", sPE, umm_done[NGRP - 1])
            wd_done[1] = p.run("gpsimd", lambda e: e.dma_start(
                out=wd_slot(1).rearrange("p (ic c) -> p ic c", ic=IC),
                in_=wd_r[1]), inc=sWDp[1], k=16)

            for hc in range(KC):
                ps = ps_d[hc % 2]
                p.wait("tensor", sWDp[hc % 2], wd_done[hc])
                if hc >= 2:
                    p.wait("tensor", sACT, devac_done[hc - 2])
                for ic in range(IC):
                    if hc == 0 and ic % 4 == 0:
                        p.wait("tensor", sHTg[ic // 4], hs_tr[ic // 4][7])
                    for n in range(2):
                        p.run("tensor", lambda e, ic=ic, n=n, hc=hc, ps=ps: e.matmul(
                            ps[:, n * 512:(n + 1) * 512],
                            wd_slot(hc)[:, ic * P:(ic + 1) * P],
                            hsT_ap(ic, n * 512, 512),
                            start=(ic == 0), stop=(ic == IC - 1)),
                            inc=sPE if (ic == IC - 1 and n == 1) else None)
                dmm_done[hc] = sPE.n
                p.wait("scalar", sPE, dmm_done[hc])
                devac_done[hc] = p.run("scalar", lambda e, hc=hc, ps=ps: e.copy(
                    downT_ap(hc), ps[:]), inc=sACT)
                if hc + 2 < KC:
                    p.wait("gpsimd", sPE, dmm_done[hc])
                    wd_done[hc + 2] = p.run("gpsimd", lambda e, hc=hc: e.dma_start(
                        out=wd_slot(hc + 2).rearrange("p (ic c) -> p ic c", ic=IC),
                        in_=wd_r[hc + 2]), inc=sWDp[hc % 2], k=16)
            prev_half_dmm = dmm_done[KC - 1]

            dtm_tr = []
            p.wait("sync", sACT, devac_done[-1])
            for hc in range(KC):
                oview = T1[:, 0:8 * H].rearrange("p (tq h) -> p tq h", tq=8)[
                    :, :, hc * P:(hc + 1) * P]
                v = p.run("sync", lambda e, hc=hc, oview=oview: e.dma_start(
                    out=oview, in_=downT_ap(hc), transpose=True), inc=sQ, k=16)
                dtm_tr.append(v)
            if half == 0:
                p.wait("scalar", sXSP, x1t_spill)
                for hc in range(KC):
                    p.wait("scalar", sQ, dtm_tr[hc])
                    next_x1t_load = p.run("scalar", lambda e, hc=hc: e.dma_start(
                        out=T3x_v[:, hc:hc + 1, :],
                        in_=scr_x1t_v[:, hc:hc + 1, HALF:2 * HALF]), inc=sXL2, k=16)
            if "dtm0" in dumps and half == 0:
                p.wait("sync", sQ, dtm_tr[-1])
                p.run("sync", lambda e: e.dma_start(out=dump_aps["dtm0"], in_=dtm_ap(0)), inc=sQ, k=16)

            if half == 0:
                p.wait("gpsimd", sPE, dmm_done[KC - 1])
                wl = p.run("gpsimd", lambda e: e.dma_start(
                    out=wgsb_v[:, 0:8, :], in_=wg_r[0][:, 0:8, :]), inc=sWGp[0], k=16)
                wh = p.run("gpsimd", lambda e: e.dma_start(
                    out=wgsb_v[:, 8:16, :], in_=wg_r[0][:, 8:16, :]), inc=sWGp[1], k=16)
                next_wg0 = (wl, wh)

            osb = SCR[:, 0:4096].bitcast(F32)
            sdump = SCR[:, 4096:6144]

            r1_load = [None] * 8
            ost = [None] * 8
            rv_of = [None] * 8
            sq2_of = [None] * 8
            p.wait("sync", sPE, prev_half_dmm)
            for i in range(2):
                r1_load[i] = p.run("sync", lambda e, i=i, half=half: e.dma_start(
                    out=r1l2[i], in_=scr_rows[half * 8 + i]), inc=sRL[i % 2], k=16)

            def ln2_stats(tql):
                tqc = half * 8 + tql
                r2 = r2l2[tql % 2]
                b = (tql % 2) * 8
                p.wait("scalar", sDVE, rv_of[tql])
                cp = p.run("scalar", lambda e, r2=r2, b=b: e.activation(
                    sdump[:], r2, AF.Copy, accum_out=lncol[:, b:b + 1]), inc=sACT)
                sq = p.run("scalar", lambda e, r2=r2, b=b: e.activation(
                    sdump[:], r2, AF.Square, accum_out=lncol[:, b + 1:b + 2]), inc=sACT)
                sq2_of[tql] = sq
                p.wait("vector", sACT, cp)
                mv = p.run("vector", lambda e, b=b: e.tensor_scalar(
                    lncol[:, b + 5:b + 6], lncol[:, b:b + 1], 1.0 / H, None, ALU.mult), inc=sVS)
                p.wait("vector", sVS, mv)
                m2 = p.run("vector", lambda e, b=b: e.tensor_scalar(
                    lncol[:, b + 2:b + 3], lncol[:, b + 5:b + 6], lncol[:, b + 5:b + 6],
                    -EPS, ALU.mult, ALU.add), inc=sVS)
                p.wait("vector", sACT, sq)
                p.wait("vector", sVS, m2)
                vpe = p.run("vector", lambda e, b=b: e.tensor_scalar(
                    lncol[:, b + 3:b + 4], lncol[:, b + 1:b + 2], 1.0 / H,
                    lncol[:, b + 2:b + 3], ALU.mult, ALU.subtract), inc=sVS)
                p.wait("scalar", sVS, vpe)
                lnv = p.run("scalar", lambda e, b=b: e.activation(
                    lncol[:, b + 4:b + 5], lncol[:, b + 3:b + 4], AF.Ln), inc=sSS)
                p.wait("scalar", sSS, lnv)
                rs = p.run("scalar", lambda e, b=b: e.activation(
                    lncol[:, b + 6:b + 7], lncol[:, b + 4:b + 5], AF.Exp, scale=-0.5), inc=sACT)
                p.wait("vector", sACT, rs)
                if tql >= 1:
                    p.wait("vector", sQ, ost[tql - 1])
                ov = p.run("vector", lambda e, r2=r2, b=b: e.tensor_scalar(
                    osb[:], r2, lncol[:, b + 5:b + 6], lncol[:, b + 6:b + 7],
                    ALU.subtract, ALU.mult), inc=sDVE)
                p.wait("sync", sDVE, ov)
                ost[tql] = p.run("sync", lambda e, tqc=tqc: e.dma_start(
                    out=out_rows[tqc], in_=osb[:]), inc=sQ, k=16)
                return ov

            for tql in range(8):
                tqc = half * 8 + tql
                p.wait("vector", sRL[tql % 2], r1_load[tql])
                if tql == 0:
                    p.wait("vector", sQ, dtm_tr[-1])
                if tql >= 2:
                    p.wait("vector", sACT, sq2_of[tql - 2])
                rv = p.run("vector", lambda e, tql=tql, tqc=tqc: e.scalar_tensor_tensor(
                    r2l2[tql % 2], r1l2[tql % 2], rcols[:, tqc:tqc + 1], dtm_ap(tql),
                    ALU.mult, ALU.add), inc=sDVE)
                rv_of[tql] = rv
                if tql + 2 < 8:
                    p.wait("sync", sDVE, rv)
                    r1_load[tql + 2] = p.run("sync", lambda e, tql=tql, half=half: e.dma_start(
                        out=r1l2[tql % 2], in_=scr_rows[half * 8 + tql + 2]), inc=sRL[tql % 2], k=16)
                if tql >= 1:
                    ov = ln2_stats(tql - 1)
            ov = ln2_stats(7)
            ln2_last_v = ov
            ln2_last_store = ost[7]
            prev_half_lnsq = sq2_of[7]


        e_psum.close()

        for s in (sQ, *sHTg, sPC, sXSP, sXL2, sRL[0], sRL[1]):
            p.wait("sync", s, s.n)
        for s in (sG, sWV, *sXC, *sWB, *sXL, sSP, *sWGp, *sWUp, *sWDp):
            p.wait("gpsimd", s, s.n)

        block = top.enter_context(nc.Block())
        p.replay(block)

    return nc



def _make_tables():
    i = np.arange(32, dtype=np.float64)
    freq = ROPE_THETA ** (-i / 32.0)
    t = np.arange(T, dtype=np.float64)
    ang = np.outer(freq, t)
    c = np.cos(ang)
    s = np.sin(ang)
    ctab = np.ones((P, T), np.float32)
    stab = np.zeros((P, T), np.float32)
    for ii in range(32):
        ctab[2 * ii] = c[ii]
        ctab[2 * ii + 1] = c[ii]
        stab[2 * ii] = -s[ii]
        stab[2 * ii + 1] = s[ii]
    permm = np.zeros((P, P), ml_dtypes.bfloat16)
    for a in range(0, P, 2):
        permm[a, a + 1] = 1
        permm[a + 1, a] = 1
    onesm = np.ones((P, P), ml_dtypes.bfloat16)
    identm = np.eye(P).astype(ml_dtypes.bfloat16)
    return ctab.astype(ml_dtypes.bfloat16), stab.astype(ml_dtypes.bfloat16), permm, onesm, identm


def _prep_weights(Wq, Wk, Wg, Wu, Wd):
    wq_r = np.ascontiguousarray(Wq.reshape(KC, P, NH, DH).transpose(2, 1, 0, 3))
    wk_r = np.ascontiguousarray(Wk.reshape(KC, P, NKV, DH).transpose(2, 1, 0, 3))
    wg_r = np.ascontiguousarray(Wg.reshape(KC, P, NGRP, GRP).transpose(2, 1, 0, 3))
    wu_r = np.ascontiguousarray(Wu.reshape(KC, P, NGRP, GRP).transpose(2, 1, 0, 3))
    wd_r = np.ascontiguousarray(Wd.reshape(IC, P, KC, P).transpose(2, 1, 0, 3))
    return wq_r, wk_r, wg_r, wu_r, wd_r


def kernel(L0_attn_hidden_states, L0_attn_Wq, L0_attn_Wk, L0_attn_Wv, L0_attn_Wo,
           L0_ffn_w_gate, L0_ffn_w_up, L0_ffn_w_down):
    hs = np.asarray(L0_attn_hidden_states, np.float32)
    Wq = np.asarray(L0_attn_Wq, np.float32)
    Wk = np.asarray(L0_attn_Wk, np.float32)
    Wv = np.asarray(L0_attn_Wv, np.float32)
    Wo = np.asarray(L0_attn_Wo, np.float32)
    Wg = np.asarray(L0_ffn_w_gate, np.float32)
    Wu = np.asarray(L0_ffn_w_up, np.float32)
    Wd = np.asarray(L0_ffn_w_down, np.float32)

    ctab, stab, permm, onesm, identm = _make_tables()
    wq_r, wk_r, wg_r, wu_r, wd_r = _prep_weights(Wq, Wk, Wg, Wu, Wd)

    nc = build_core_program()
    base = {
        "wq_r": wq_r, "wk_r": wk_r, "wv": np.ascontiguousarray(Wv),
        "wo": np.ascontiguousarray(Wo),
        "wg_r": wg_r, "wu_r": wu_r, "wd_r": wd_r,
        "ctab": ctab, "stab": stab, "perm": permm, "onesm": onesm, "ident": identm,
    }
    in_maps = []
    for core in range(8):
        m = dict(base)
        m["x"] = np.ascontiguousarray(hs[core])
        in_maps.append(m)
    res = run_bass_kernel_spmd(nc, in_maps, core_ids=list(range(8)))
    out = np.stack([r["out"] for r in res.results], axis=0)
    return out.astype(np.float32)


# version 14
# speedup vs baseline: 1.1115x; 1.1115x over previous
import math
from contextlib import ExitStack

import numpy as np
import ml_dtypes

import concourse.bass as bass
import concourse.mybir as mybir
from concourse.bass_utils import run_bass_kernel_spmd

F32 = mybir.dt.float32
BF16 = mybir.dt.bfloat16
AF = mybir.ActivationFunctionType
ALU = mybir.AluOpType
AX = mybir.AxisListType

P = 128
T = 2048
H = 2048
TC = T // P
KC = H // P
NH, NKV, DH = 16, 4, 128
I = 8192
IC = I // P
GRP = 512
NGRP = I // GRP
HALF = 1024
ROPE_THETA = 10000.0
SCALE = 1.0 / math.sqrt(DH)
EPS = 1e-5

ENGINES = ("sync", "gpsimd", "tensor", "scalar", "vector")


class Sem:
    def __init__(self, handle):
        self.h = handle
        self.n = 0

    def bump(self, k):
        self.n += k
        return self.n


class Prog:

    def __init__(self):
        self.streams = {e: [] for e in ENGINES}

    def wait(self, eng, sem, val):
        if val is None:
            return
        self.streams[eng].append(lambda e, s=sem.h, v=val: e.wait_ge(s, v))

    def run(self, eng, fn, inc=None, k=1):
        if inc is None:
            self.streams[eng].append(fn)
            return None
        v = inc.bump(k)
        self.streams[eng].append(lambda e, f=fn, s=inc.h, kk=k: f(e).then_inc(s, kk))
        return v

    def replay(self, block):
        for name in ENGINES:
            stream = self.streams[name]

            def make(st):
                def body(eng):
                    for fn in st:
                        fn(eng)
                return body
            getattr(block, name)(make(stream))


def build_core_program(dumps=()):
    dumps = set(dumps)
    nc = bass.Bass("TRN2", target_bir_lowering=False, detect_race_conditions=False)

    x_in = nc.dram_tensor("x", [T, H], F32, kind="ExternalInput").ap()
    wq_r = nc.dram_tensor("wq_r", [NH, P, KC, DH], F32, kind="ExternalInput").ap()
    wk_r = nc.dram_tensor("wk_r", [NKV, P, KC, DH], F32, kind="ExternalInput").ap()
    wv_in = nc.dram_tensor("wv", [H, NKV * DH], F32, kind="ExternalInput").ap()
    wo_in = nc.dram_tensor("wo", [H, H], F32, kind="ExternalInput").ap()
    wg_r = nc.dram_tensor("wg_r", [NGRP, P, KC, GRP], F32, kind="ExternalInput").ap()
    wu_r = nc.dram_tensor("wu_r", [NGRP, P, KC, GRP], F32, kind="ExternalInput").ap()
    wd_r = nc.dram_tensor("wd_r", [KC, P, IC, P], F32, kind="ExternalInput").ap()
    ctab_in = nc.dram_tensor("ctab", [P, T], BF16, kind="ExternalInput").ap()
    stab_in = nc.dram_tensor("stab", [P, T], BF16, kind="ExternalInput").ap()
    perm_in = nc.dram_tensor("perm", [P, P], BF16, kind="ExternalInput").ap()
    ones_in = nc.dram_tensor("onesm", [P, P], BF16, kind="ExternalInput").ap()
    ident_in = nc.dram_tensor("ident", [P, P], BF16, kind="ExternalInput").ap()
    out_hbm = nc.dram_tensor("out", [T, H], F32, kind="ExternalOutput").ap()
    scr_r1 = nc.dram_tensor("scr_r1", [T, H], BF16, kind="Internal").ap()
    scr_x1t = nc.dram_tensor("scr_x1t", [P, KC * T], BF16, kind="Internal").ap()

    dump_aps = {}
    _dump_specs = {
        "xT": ([P, KC * T], BF16), "q0": ([P, T], BF16), "k0": ([P, T], BF16),
        "vall": ([P, TC * 512], BF16), "oallt": ([P, NH * T], BF16),
        "z0": ([P, 1024], F32), "x1bf": ([P, TC * H], BF16),
        "x1t": ([P, KC * T], BF16), "hst0": ([P, HALF], BF16), "dtm0": ([P, H], BF16),
        "gs0": ([P, GRP], BF16), "hb0": ([P, GRP], BF16),
    }
    for tag in dumps:
        shp, dt = _dump_specs[tag]
        dump_aps[tag] = nc.dram_tensor("dbg_" + tag, shp, dt, kind="ExternalOutput").ap()

    p = Prog()

    with ExitStack() as top:
        def sem(name):
            return Sem(top.enter_context(nc.semaphore(name)))

        T1 = top.enter_context(nc.sbuf_tensor("T1", [P, 32768], BF16))
        T2 = top.enter_context(nc.sbuf_tensor("T2", [P, 32768], BF16))
        T3 = top.enter_context(nc.sbuf_tensor("T3", [P, 32768], BF16))

        perm_sb = top.enter_context(nc.sbuf_tensor("perm_sb", [P, P], BF16))
        ones_sb = top.enter_context(nc.sbuf_tensor("ones_sb", [P, P], BF16))
        ident_sb = top.enter_context(nc.sbuf_tensor("ident_sb", [P, P], BF16))
        mcols = top.enter_context(nc.sbuf_tensor("mcols", [P, TC], F32))
        rcols = top.enter_context(nc.sbuf_tensor("rcols", [P, TC], F32))
        lncol = top.enter_context(nc.sbuf_tensor("lncol", [P, 16], F32))
        SCR = top.enter_context(nc.sbuf_tensor("SCR", [P, 7552], BF16))

        sQ = sem("sQ")
        sG = sem("sG")
        sPE = sem("sPE")
        sACT = sem("sACT")
        sDVE = sem("sDVE")
        sVS = sem("sVS")
        sSS = sem("sSS")
        sWV = sem("sWV")
        sXC = [sem(f"sXC{i}") for i in range(4)]
        sWB = [sem(f"sWB{i}") for i in range(2)]
        sXL = [sem(f"sXL{i}") for i in range(8)]
        sSP = sem("sSP")
        sWGp = [sem(f"sWG{i}") for i in range(2)]
        sWUp = [sem(f"sWU{i}") for i in range(2)]
        sWDp = [sem(f"sWD{i}") for i in range(2)]
        sHTg = [sem(f"sHTg{i}") for i in range(NGRP)]
        sPC = sem("sPC")
        sXSP = sem("sXSP")
        sXL2 = sem("sXTL")
        sRL = [sem(f"sRL{i}") for i in range(2)]

        def t_slot(tens, slot):
            return tens[:, slot * T:(slot + 1) * T]

        K_OFF, V_OFF, QSB_OFF = 0, 8192, 16384
        T1R_OFF, T2R_OFF, WDB_OFF, CT_OFF, ST_OFF = 18432, 20480, 22528, 26624, 28672
        EXP_OFF = QSB_OFF

        def kslot(g):
            return T3[:, K_OFF + g * T: K_OFF + (g + 1) * T]

        def vslot(tkc):
            return T3[:, V_OFF + tkc * 512: V_OFF + (tkc + 1) * 512]

        qsb2 = [T3[:, QSB_OFF:QSB_OFF + T], T3[:, 30720:32768]]
        t1_ap = T3[:, T1R_OFF:T1R_OFF + T]
        t2_ap = T3[:, T2R_OFF:T2R_OFF + T]
        ctab_sb = T3[:, CT_OFF:CT_OFF + T]
        stab_sb = T3[:, ST_OFF:ST_OFF + T]

        def wdbslot(i):
            o = WDB_OFF + (i % 2) * 2048
            return T3[:, o:o + 2048]

        def expslot(i):
            o = EXP_OFF + (i % 3) * 1024
            return T3[:, o:o + 1024]

        X1T_OFF, WG_OFF, WU_OFF = 0, 16384, 24576

        def x1t_ap(kc, lo, n):
            o = X1T_OFF + kc * HALF + lo
            return T3[:, o:o + n]

        def wg_ap(kc):
            return T3[:, WG_OFF + kc * GRP: WG_OFF + (kc + 1) * GRP]

        def wu_ap(kc):
            return T3[:, WU_OFF + kc * GRP: WU_OFF + (kc + 1) * GRP]

        def wd_slot(i):
            o = WG_OFF + (i % 2) * 8192
            return T3[:, o:o + 8192]

        def hsT_ap(ic, lo=0, n=HALF):
            tens = T1 if ic < 32 else T2
            o = (ic % 32) * HALF + lo
            return tens[:, o:o + n]

        def downT_ap(hc, lo=0, n=HALF):
            return T3[:, hc * HALF + lo: hc * HALF + lo + n]

        def dtm_ap(tql, lo=0, n=H):
            return T1[:, tql * H + lo: tql * H + lo + n]

        r1l2 = [T2[:, 0:H], T2[:, H:2 * H]]
        r2l2 = [T2[:, 2 * H:3 * H], T2[:, 3 * H:4 * H]]

        xin_v = x_in.rearrange("(tc p) h -> p tc h", p=P)
        xbf_v = T2[:].rearrange("p (tc h) -> p tc h", tc=TC)
        wv_v = wv_in.rearrange("(kc p) c -> p kc c", p=P)
        wvsb_v = T3[:, 0:8192].rearrange("p (kc c) -> p kc c", kc=KC)
        wv_done = p.run("gpsimd", lambda e: e.dma_start(out=wvsb_v, in_=wv_v), inc=sWV, k=16)

        xcast_done = [None] * 4
        for i in range(4):
            xcast_done[i] = p.run("gpsimd", lambda e, i=i: e.dma_start(
                out=xbf_v[:, i * 4:(i + 1) * 4, :], in_=xin_v[:, i * 4:(i + 1) * 4, :]),
                inc=sXC[i], k=16)

        p.run("sync", lambda e: e.dma_start(out=ctab_sb, in_=ctab_in), inc=sQ, k=16)
        p.run("sync", lambda e: e.dma_start(out=stab_sb, in_=stab_in), inc=sQ, k=16)
        p.run("sync", lambda e: e.dma_start(out=perm_sb[:], in_=perm_in), inc=sQ, k=16)
        p.run("sync", lambda e: e.dma_start(out=ones_sb[:], in_=ones_in), inc=sQ, k=16)
        tabs_done = p.run("sync", lambda e: e.dma_start(out=ident_sb[:], in_=ident_in), inc=sQ, k=16)

        T1_hview = T1[:].rearrange("p (hc t) -> p hc t", hc=KC)
        a_psum = ExitStack()
        a_ps = [a_psum.enter_context(nc.psum_tensor(f"a_ps{i}", [P, H], BF16)) for i in range(2)]
        xT_done = []
        prev_evc = [None, None]
        p.wait("tensor", sQ, tabs_done)
        for tc in range(TC):
            aps = a_ps[tc % 2]
            p.wait("tensor", sXC[tc // 4], xcast_done[tc // 4])
            p.wait("tensor", sACT, prev_evc[tc % 2])
            for hc in range(KC):
                p.run("tensor", lambda e, tc=tc, hc=hc, aps=aps: e.transpose(
                    aps[:, hc * P:(hc + 1) * P],
                    T2[:, tc * H + hc * P: tc * H + (hc + 1) * P], ident_sb[:]),
                    inc=sPE if hc == KC - 1 else None)
            trd = sPE.n
            p.wait("scalar", sPE, trd)
            v = p.run("scalar", lambda e, tc=tc, aps=aps: e.copy(
                T1_hview[:, :, tc * P:(tc + 1) * P], aps[:]), inc=sACT)
            prev_evc[tc % 2] = v
            xT_done.append(v)

        if "xT" in dumps:
            p.wait("sync", sACT, xT_done[-1])
            p.run("sync", lambda e: e.dma_start(out=dump_aps["xT"], in_=T1[:]), inc=sQ, k=16)

        a_psum.close()
        b_psum = ExitStack()
        ps_bq = b_psum.enter_context(nc.psum_tensor("ps_bq", [P, T], F32))
        ps_bp = b_psum.enter_context(nc.psum_tensor("ps_bp", [P, T], F32))

        p.wait("tensor", sACT, xT_done[-1])
        p.wait("tensor", sWV, wv_done)
        v_evac = [None] * TC
        for tkc in range(TC):
            sl = ps_bq[:, (tkc % 4) * 512:(tkc % 4) * 512 + 512]
            if tkc >= 4:
                p.wait("tensor", sACT, v_evac[tkc - 4])
            for kc in range(KC):
                p.run("tensor", lambda e, kc=kc, tkc=tkc, sl=sl: e.matmul(
                    sl, T1[:, kc * T + tkc * P: kc * T + (tkc + 1) * P],
                    wvsb_v[:, kc, :], start=(kc == 0), stop=(kc == KC - 1)),
                    inc=sPE if kc == KC - 1 else None)
            mmv = sPE.n
            p.wait("scalar", sPE, mmv)
            v_evac[tkc] = p.run("scalar", lambda e, tkc=tkc, sl=sl: e.copy(vslot(tkc), sl), inc=sACT)

        if "vall" in dumps:
            p.wait("sync", sACT, v_evac[-1])
            p.run("sync", lambda e: e.dma_start(
                out=dump_aps["vall"], in_=T3[:, V_OFF:V_OFF + TC * 512]), inc=sQ, k=16)

        head_rope_done = [None] * NH
        kv_rope_done = [None] * NKV
        trk = {
            "prev_wmm": [None, None],
            "prev_evac": v_evac[TC - 1],
            "prev_t2": None,
            "prev_t2_par": [None, None],
        }

        def emit_proj_rope(idx, wr_dram, dst_ap, done_list):
            wsl = wdbslot(idx)
            wsb_v = wsl.rearrange("p (kc c) -> p kc c", kc=KC)
            win_v = wr_dram[idx]
            qs = qsb2[idx % 2]
            if trk["prev_wmm"][idx % 2] is not None:
                p.wait("gpsimd", sPE, trk["prev_wmm"][idx % 2])
            wdone = p.run("gpsimd", lambda e, wsb_v=wsb_v, win_v=win_v: e.dma_start(
                out=wsb_v, in_=win_v), inc=sWB[idx % 2], k=16)

            p.wait("tensor", sWB[idx % 2], wdone)
            p.wait("tensor", sACT, trk["prev_evac"])
            for kc in range(KC):
                for n in range(4):
                    p.run("tensor", lambda e, kc=kc, n=n, wsb_v=wsb_v: e.matmul(
                        ps_bq[:, n * 512:(n + 1) * 512], wsb_v[:, kc, :],
                        T1[:, kc * T + n * 512: kc * T + (n + 1) * 512],
                        start=(kc == 0), stop=(kc == KC - 1)),
                        inc=sPE if (kc == KC - 1 and n == 3) else None)
            mm_done = sPE.n
            trk["prev_wmm"][idx % 2] = mm_done

            p.wait("scalar", sPE, mm_done)
            p.wait("scalar", sVS, trk["prev_t2_par"][idx % 2])
            evac = p.run("scalar", lambda e, qs=qs: e.copy(qs, ps_bq[:]), inc=sACT)
            trk["prev_evac"] = evac

            p.wait("tensor", sACT, evac)
            p.wait("tensor", sVS, trk["prev_t2"])
            for n in range(4):
                p.run("tensor", lambda e, n=n, qs=qs: e.matmul(
                    ps_bp[:, n * 512:(n + 1) * 512], perm_sb[:],
                    qs[:, n * 512:(n + 1) * 512], start=True, stop=True),
                    inc=sPE if n == 3 else None)
            pm_done = sPE.n

            p.wait("vector", sACT, evac)
            p.run("vector", lambda e, qs=qs: e.tensor_mul(t1_ap, qs, ctab_sb))
            p.wait("vector", sPE, pm_done)
            t2v = p.run("vector", lambda e: e.tensor_mul(t2_ap, ps_bp[:], stab_sb), inc=sVS)
            trk["prev_t2"] = t2v
            trk["prev_t2_par"][idx % 2] = t2v
            p.wait("vector", sVS, t2v)
            done_list[idx] = p.run("vector", lambda e, d=dst_ap: e.tensor_add(d, t1_ap, t2_ap), inc=sDVE)

        p.wait("tensor", sQ, tabs_done)
        p.wait("vector", sQ, tabs_done)
        for g in range(NKV):
            emit_proj_rope(g, wk_r, kslot(g), kv_rope_done)
        for h in range(NH):
            emit_proj_rope(h, wq_r, t_slot(T2, h), head_rope_done)

        if "q0" in dumps:
            p.wait("sync", sDVE, head_rope_done[0])
            p.run("sync", lambda e: e.dma_start(out=dump_aps["q0"], in_=t_slot(T2, 0)), inc=sQ, k=16)
        if "k0" in dumps:
            p.wait("sync", sDVE, kv_rope_done[0])
            p.run("sync", lambda e: e.dma_start(out=dump_aps["k0"], in_=kslot(0)), inc=sQ, k=16)

        b_psum.close()

        c_psum = ExitStack()
        ps_sc = c_psum.enter_context(nc.psum_tensor("ps_sc", [P, 3072], F32))
        ps_pv = c_psum.enter_context(nc.psum_tensor("ps_pv", [P, 1024], F32))
        zk = [SCR[:, i * 1024:(i + 1) * 1024] for i in range(2)]
        zrec = SCR[:, 2048:4096].bitcast(F32)

        p.wait("tensor", sVS, trk["prev_t2"])
        p.wait("scalar", sVS, trk["prev_t2"])
        p.wait("tensor", sDVE, kv_rope_done[NKV - 1])

        head_done = [None] * NH
        last_exp3 = [None, None, None]
        last_z3 = [None, None, None]
        prev_pv_evac = None
        prev_zrec = None
        for h in range(NH):
            g = h // NKV
            p.wait("tensor", sDVE, head_rope_done[h])
            for slab in range(2):
                qslab = t_slot(T2, h)[:, slab * 1024:(slab + 1) * 1024]
                exp_of = {}
                zadd_of = {}

                def scores(c):
                    third = c % 3
                    p.wait("tensor", sACT, last_exp3[third])
                    if third == 2:
                        p.wait("tensor", sSS, prev_zrec)
                    for n in range(2):
                        p.run("tensor", lambda e, c=c, n=n, third=third, qslab=qslab, g=g: e.matmul(
                            ps_sc[:, third * 1024 + n * 512: third * 1024 + (n + 1) * 512],
                            kslot(g)[:, c * P:(c + 1) * P],
                            qslab[:, n * 512:(n + 1) * 512], start=True, stop=True),
                            inc=sPE if n == 1 else None)
                    sc_done = sPE.n
                    p.wait("scalar", sPE, sc_done)
                    p.wait("scalar", sVS, last_z3[third])
                    ev = p.run("scalar", lambda e, c=c, third=third: e.activation(
                        expslot(c), ps_sc[:, third * 1024:(third + 1) * 1024],
                        AF.Exp, scale=SCALE), inc=sACT)
                    last_exp3[third] = ev
                    exp_of[c] = ev
                    p.wait("vector", sACT, ev)
                    if c == 0:
                        zv = p.run("vector", lambda e, c=c: e.tensor_copy(
                            zk[0][:], expslot(c)), inc=sVS)
                    else:
                        p.wait("vector", sVS, zadd_of[c - 1])
                        zv = p.run("vector", lambda e, c=c: e.tensor_add(
                            zk[c % 2][:], zk[(c - 1) % 2][:], expslot(c)), inc=sVS)
                    last_z3[third] = zv
                    zadd_of[c] = zv

                def pv(c):
                    p.wait("tensor", sACT, exp_of[c])
                    if c == 0 and prev_pv_evac is not None:
                        p.wait("tensor", sDVE, prev_pv_evac)
                    for n in range(2):
                        p.run("tensor", lambda e, c=c, n=n, g=g: e.matmul(
                            ps_pv[:, n * 512:(n + 1) * 512],
                            vslot(c)[:, g * P:(g + 1) * P],
                            expslot(c)[:, n * 512:(n + 1) * 512],
                            start=(c == 0), stop=(c == TC - 1)),
                            inc=sPE if (n == 1 and c == TC - 1) else None)

                scores(0)
                scores(1)
                scores(2)
                for c in range(3, TC):
                    pv(c - 3)
                    scores(c)
                pv(TC - 3)
                pv(TC - 2)
                p.wait("tensor", sVS, zadd_of[TC - 1])
                for n in range(2):
                    p.run("tensor", lambda e, n=n: e.matmul(
                        ps_sc[:, 2048 + n * 512: 2048 + (n + 1) * 512], ones_sb[:],
                        zk[(TC - 1) % 2][:, n * 512:(n + 1) * 512],
                        start=True, stop=True), inc=sPE if n == 1 else None)
                zmm_done = sPE.n
                pv(TC - 1)
                pv_done = sPE.n

                p.wait("scalar", sPE, zmm_done)
                p.wait("scalar", sDVE, prev_pv_evac)
                lnz = p.run("scalar", lambda e: e.activation(
                    zrec[:], ps_sc[:, 2048:3072], AF.Ln), inc=sSS)
                p.wait("scalar", sSS, lnz)
                zrv = p.run("scalar", lambda e: e.activation(
                    zrec[:], zrec[:], AF.Exp, scale=-1.0), inc=sACT)
                p.wait("vector", sPE, pv_done)
                p.wait("vector", sACT, zrv)
                dst = t_slot(T1, h)[:, slab * 1024:(slab + 1) * 1024]
                prev_pv_evac = p.run("vector", lambda e, dst=dst: e.tensor_mul(
                    dst, ps_pv[:], zrec[:]), inc=sDVE)
                prev_zrec = lnz
                if "z0" in dumps and h == 0 and slab == 0:
                    p.wait("sync", sACT, zrv)
                    p.run("sync", lambda e: e.dma_start(
                        out=dump_aps["z0"], in_=zrec[:]), inc=sQ, k=16)
            head_done[h] = prev_pv_evac

            wo_v = wo_in.rearrange("(kc p) c -> p kc c", p=P)[:, h, :]
            p.wait("gpsimd", sDVE, head_done[h])
            p.run("gpsimd", lambda e, h=h, wo_v=wo_v: e.dma_start(
                out=t_slot(T2, h), in_=wo_v), inc=sG, k=16)
        wo_loaded = sG.n

        if "oallt" in dumps:
            p.wait("sync", sDVE, head_done[-1])
            p.run("sync", lambda e: e.dma_start(out=dump_aps["oallt"], in_=T1[:]), inc=sQ, k=16)

        c_psum.close()

        d_psum = ExitStack()
        ps_wo = [d_psum.enter_context(nc.psum_tensor(f"ps_wo{i}", [P, H], F32)) for i in range(2)]
        r1_sb = SCR[:, 0:4096].bitcast(F32)

        scr_rows = scr_r1.rearrange("(tc p) h -> tc p h", p=P)
        out_rows = out_hbm.rearrange("(tc p) h -> tc p h", p=P)
        xin_rows = x_in.rearrange("(tc p) h -> tc p h", p=P)

        xload_done = [None] * TC
        for tqc in range(8):
            xload_done[tqc] = p.run("gpsimd", lambda e, tqc=tqc: e.dma_start(
                out=t_slot(T3, tqc), in_=xin_rows[tqc]), inc=sXL[tqc % 8], k=16)

        r1_done = [None] * TC
        r1_spill = [None] * TC
        sq_done = [None] * TC
        x1bf_done = [None] * TC
        womm_done = [None] * TC
        p.wait("tensor", sG, wo_loaded)
        for tqc in range(TC):
            ps = ps_wo[tqc % 2]
            if tqc >= 2:
                p.wait("tensor", sDVE, r1_done[tqc - 2])
            for kc in range(KC):
                for n in range(4):
                    p.run("tensor", lambda e, kc=kc, n=n, tqc=tqc, ps=ps: e.matmul(
                        ps[:, n * 512:(n + 1) * 512],
                        t_slot(T1, kc)[:, tqc * P:(tqc + 1) * P],
                        t_slot(T2, kc)[:, n * 512:(n + 1) * 512],
                        start=(kc == 0), stop=(kc == KC - 1)),
                        inc=sPE if (kc == KC - 1 and n == 3) else None)
            wo_done = sPE.n
            womm_done[tqc] = wo_done

            p.wait("vector", sXL[tqc % 8], xload_done[tqc])
            p.wait("vector", sPE, wo_done)
            if tqc >= 1:
                p.wait("vector", sSP, r1_spill[tqc - 1])
                p.wait("vector", sACT, sq_done[tqc - 1])
            rd = p.run("vector", lambda e, ps=ps, tqc=tqc: e.tensor_add(
                r1_sb[:], t_slot(T3, tqc), ps[:]), inc=sDVE)
            r1_done[tqc] = rd

            p.wait("gpsimd", sDVE, rd)
            r1_spill[tqc] = p.run("gpsimd", lambda e, tqc=tqc: e.dma_start(
                out=scr_rows[tqc], in_=r1_sb[:]), inc=sSP, k=16)
            if tqc + 8 < TC:
                xload_done[tqc + 8] = p.run("gpsimd", lambda e, t=tqc + 8: e.dma_start(
                    out=t_slot(T3, t), in_=xin_rows[t]), inc=sXL[tqc % 8], k=16)

            p.wait("scalar", sDVE, rd)
            cp = p.run("scalar", lambda e, tqc=tqc: e.activation(
                t_slot(T3, tqc), r1_sb[:], AF.Copy, accum_out=lncol[:, 1:2]), inc=sACT)
            sq = p.run("scalar", lambda e, tqc=tqc: e.activation(
                t_slot(T3, tqc), r1_sb[:], AF.Square, accum_out=lncol[:, 0:1]), inc=sACT)
            sq_done[tqc] = sq
            p.wait("vector", sACT, cp)
            mv = p.run("vector", lambda e, tqc=tqc: e.tensor_scalar(
                mcols[:, tqc:tqc + 1], lncol[:, 1:2], 1.0 / H, None, ALU.mult), inc=sVS)
            p.wait("vector", sVS, mv)
            m2 = p.run("vector", lambda e, tqc=tqc: e.tensor_scalar(
                lncol[:, 2:3], mcols[:, tqc:tqc + 1], mcols[:, tqc:tqc + 1], -EPS,
                ALU.mult, ALU.add), inc=sVS)
            p.wait("vector", sACT, sq)
            p.wait("vector", sVS, m2)
            vpe = p.run("vector", lambda e: e.tensor_scalar(
                lncol[:, 3:4], lncol[:, 0:1], 1.0 / H, lncol[:, 2:3],
                ALU.mult, ALU.subtract), inc=sVS)
            p.wait("scalar", sVS, vpe)
            lnv = p.run("scalar", lambda e: e.activation(lncol[:, 4:5], lncol[:, 3:4], AF.Ln), inc=sSS)
            p.wait("scalar", sSS, lnv)
            rs = p.run("scalar", lambda e, tqc=tqc: e.activation(
                rcols[:, tqc:tqc + 1], lncol[:, 4:5], AF.Exp, scale=-0.5), inc=sACT)
            p.wait("vector", sACT, rs)
            x1bf_done[tqc] = p.run("vector", lambda e, tqc=tqc: e.tensor_scalar(
                t_slot(T3, tqc), r1_sb[:], mcols[:, tqc:tqc + 1], rcols[:, tqc:tqc + 1],
                ALU.subtract, ALU.mult), inc=sDVE)

        if "x1bf" in dumps:
            p.wait("sync", sDVE, x1bf_done[-1])
            p.run("sync", lambda e: e.dma_start(out=dump_aps["x1bf"], in_=T3[:]), inc=sQ, k=16)

        d_psum.close()

        T2_hview = T2[:].rearrange("p (kc t) -> p kc t", kc=KC)
        dx_stack = ExitStack()
        dx_pad = dx_stack.enter_context(nc.psum_tensor("dx_pad", [P, H], F32))
        dx_ps = [dx_stack.enter_context(nc.psum_tensor(f"dx_ps{i}", [P, H], BF16)) for i in range(2)]
        x1t_evac = [None] * TC
        p.wait("tensor", sDVE, r1_done[15])
        for tc in range(TC):
            dps = dx_ps[tc % 2]
            p.wait("tensor", sDVE, x1bf_done[tc])
            p.wait("tensor", sACT, x1t_evac[tc - 2] if tc >= 2 else None)
            for hc in range(KC):
                p.run("tensor", lambda e, tc=tc, hc=hc, dps=dps: e.transpose(
                    dps[:, hc * P:(hc + 1) * P],
                    T3[:, tc * H + hc * P: tc * H + (hc + 1) * P], ident_sb[:]),
                    inc=sPE if hc == KC - 1 else None)
            trd = sPE.n
            p.wait("scalar", sPE, trd)
            x1t_evac[tc] = p.run("scalar", lambda e, tc=tc, dps=dps: e.copy(
                T2_hview[:, :, tc * P:(tc + 1) * P], dps[:]), inc=sACT)
        wgfree = x1t_evac[11]
        wufree = x1t_evac[15]
        x1t_pieces = []
        T3x_v = T3[:, 0:KC * HALF].rearrange("p (kc t) -> p kc t", kc=KC)
        p.wait("sync", sACT, x1t_evac[7])
        for blk in range(2):
            v = p.run("sync", lambda e, blk=blk: e.dma_start(
                out=T3x_v[:, :, blk * 512:(blk + 1) * 512],
                in_=T2_hview[:, :, blk * 512:(blk + 1) * 512]), inc=sPC, k=16)
            x1t_pieces.append(v)
        p.wait("sync", sACT, x1t_evac[15])
        x1t_spill = p.run("sync", lambda e: e.dma_start(out=scr_x1t, in_=T2[:]), inc=sXSP, k=16)
        if "x1t" in dumps:
            p.run("sync", lambda e: e.dma_start(out=dump_aps["x1t"], in_=T2[:]), inc=sQ, k=16)
        dx_stack.close()

        e_psum = ExitStack()
        ps_g4 = [e_psum.enter_context(nc.psum_tensor(f"ps_g{i}", [P, GRP], F32)) for i in range(4)]
        ps_d = [e_psum.enter_context(nc.psum_tensor(f"ps_d{i}", [P, HALF], F32)) for i in range(2)]

        scr_x1t_v = scr_x1t.rearrange("p (kc t) -> p kc t", kc=KC)
        wgsb_v = T3[:, WG_OFF:WG_OFF + KC * GRP].rearrange("p (kc c) -> p kc c", kc=KC)
        wusb_v = T3[:, WU_OFF:WU_OFF + KC * GRP].rearrange("p (kc c) -> p kc c", kc=KC)

        prev_half_dmm = None
        prev_half_hb = None
        prev_half_hstr = None
        prev_half_lnsq = None
        ln2_last_v = None
        ln2_last_store = None
        next_x1t_load = None
        next_wg0 = None
        for half in range(2):
            x1t_load = None if half == 0 else next_x1t_load

            gsilu = SCR[:, 0:8 * GRP]
            gsilu_g0 = T2[:, 4096:8192]

            def gs_ap(grp):
                return gsilu_g0 if (half == 1 and grp == 0) else gsilu
            hbuf2 = [SCR[:, 4096 + i * GRP: 4096 + (i + 1) * GRP] for i in range(2)]
            hsq2 = [SCR[:, 5120 + i * GRP: 5120 + (i + 1) * GRP] for i in range(2)]
            hs_sb = SCR[:, 6144:6144 + GRP]
            mmx = SCR[:, 6656:6656 + 7 * (GRP // 4)]

            wg_done = [None] * NGRP
            wu_done = [None] * NGRP
            gmm_done = [None] * NGRP
            umm_done = [None] * NGRP
            silu_done = [[None] * 8 for _ in range(NGRP)]
            hb_done = [[None] * 8 for _ in range(NGRP)]
            hs_tr = [[None] * 8 for _ in range(NGRP)]
            sq_done2 = [[None] * 8 for _ in range(NGRP)]
            cmp_done2 = [[None] * 8 for _ in range(NGRP)]

            if half == 0:
                p.wait("gpsimd", sACT, wgfree)
                wl = p.run("gpsimd", lambda e: e.dma_start(
                    out=wgsb_v[:, 0:8, :], in_=wg_r[0][:, 0:8, :]), inc=sWGp[0], k=16)
                wh = p.run("gpsimd", lambda e: e.dma_start(
                    out=wgsb_v[:, 8:16, :], in_=wg_r[0][:, 8:16, :]), inc=sWGp[1], k=16)
                wg_done[0] = (wl, wh)
            else:
                wg_done[0] = next_wg0

            for grp in range(NGRP):
                if half == 0 and grp < 8:
                    xap = lambda kc, tq: T2[:, kc * T + tq * P: kc * T + tq * P + P]
                else:
                    xap = lambda kc, tq: x1t_ap(kc, tq * P, P)
                for tq in range(8):
                    if grp == 0:
                        if half == 0:
                            p.wait("tensor", sACT, x1t_evac[tq])
                            p.wait("tensor", sPC, x1t_pieces[-1])
                        elif tq == 0:
                            p.wait("tensor", sXL2, x1t_load)
                    if half == 0 and grp == 8 and tq == 0:
                        p.wait("tensor", sPC, x1t_pieces[-1])
                    ps = ps_g4[tq % 4]
                    if tq >= 4:
                        p.wait("tensor", sACT, silu_done[grp][tq - 4])
                    elif grp > 0:
                        p.wait("tensor", sDVE, hb_done[grp - 1][tq + 4])
                    elif half == 1:
                        p.wait("tensor", sDVE, prev_half_hb)
                    for kc in range(KC):
                        if tq == 0 and kc in (0, 8):
                            p.wait("tensor", sWGp[kc // 8], wg_done[grp][kc // 8])
                        p.run("tensor", lambda e, kc=kc, tq=tq, ps=ps: e.matmul(
                            ps[:], xap(kc, tq), wg_ap(kc),
                            start=(kc == 0), stop=(kc == KC - 1)),
                            inc=sPE if kc == KC - 1 else None)
                    gd = sPE.n
                    p.wait("scalar", sPE, gd)
                    if grp > 0:
                        p.wait("scalar", sDVE, hb_done[grp - 1][tq])
                        if half == 1 and grp == 1 and tq == 0:
                            p.wait("scalar", sQ, ln2_last_store)
                    elif half == 0:
                        p.wait("scalar", sDVE, x1bf_done[TC - 1])
                        p.wait("scalar", sSP, r1_spill[TC - 1])
                    silu_done[grp][tq] = p.run("scalar", lambda e, tq=tq, ps=ps, gs=gs_ap(grp): e.activation(
                        gs[:, tq * GRP:(tq + 1) * GRP], ps[:], AF.Silu), inc=sACT)
                    if "gs0" in dumps and half == 0 and grp == 0 and tq == 0:
                        p.wait("sync", sACT, silu_done[grp][tq])
                        p.run("sync", lambda e: e.dma_start(out=dump_aps["gs0"], in_=gsilu[:, 0:GRP]), inc=sQ, k=16)
                gmm_done[grp] = sPE.n
                if grp > 0:
                    p.wait("gpsimd", sPE, umm_done[grp - 1])
                elif half == 0:
                    p.wait("gpsimd", sACT, wufree)
                wul = p.run("gpsimd", lambda e, grp=grp: e.dma_start(
                    out=wusb_v[:, 0:8, :], in_=wu_r[grp][:, 0:8, :]), inc=sWUp[0], k=16)
                wuh = p.run("gpsimd", lambda e, grp=grp: e.dma_start(
                    out=wusb_v[:, 8:16, :], in_=wu_r[grp][:, 8:16, :]), inc=sWUp[1], k=16)
                wu_done[grp] = (wul, wuh)

                Q4 = GRP // 4
                sq_of = sq_done2[grp]
                cmp_of = cmp_done2[grp]

                def sparsify(tq):
                    s_buf = hsq2[tq % 2]
                    hb_buf = hbuf2[tq % 2]
                    p.wait("vector", sACT, sq_of[tq])
                    s2a = s_buf[:].rearrange("p (gh two) -> p gh two", two=2)
                    lob = mmx[:, 0:2 * Q4]
                    hib = mmx[:, 2 * Q4:4 * Q4]
                    mx = mmx[:, 4 * Q4:5 * Q4]
                    mn = mmx[:, 5 * Q4:6 * Q4]
                    tt = mmx[:, 6 * Q4:7 * Q4]
                    mask = mmx[:, 0:4 * Q4]
                    p.run("vector", lambda e, s2a=s2a, lob=lob: e.tensor_tensor(
                        lob, s2a[:, :, 0], s2a[:, :, 1], ALU.min))
                    v2 = p.run("vector", lambda e, s2a=s2a, hib=hib: e.tensor_tensor(
                        hib, s2a[:, :, 0], s2a[:, :, 1], ALU.max), inc=sVS)
                    p.wait("vector", sVS, v2)
                    lob2 = lob.rearrange("p (g two) -> p g two", two=2)
                    hib2 = hib.rearrange("p (g two) -> p g two", two=2)
                    p.run("vector", lambda e, lob2=lob2, mx=mx: e.tensor_tensor(
                        mx, lob2[:, :, 0], lob2[:, :, 1], ALU.max))
                    v3 = p.run("vector", lambda e, hib2=hib2, mn=mn: e.tensor_tensor(
                        mn, hib2[:, :, 0], hib2[:, :, 1], ALU.min), inc=sVS)
                    p.wait("vector", sVS, v3)
                    v4 = p.run("vector", lambda e, mx=mx, mn=mn, tt=tt: e.tensor_tensor(
                        tt, mx, mn, ALU.min), inc=sVS)
                    p.wait("vector", sVS, v4)
                    sview = s_buf[:].rearrange("p (g four) -> p g four", four=4)
                    mview = mask.rearrange("p (g four) -> p g four", four=4)
                    for j in range(4):
                        p.run("vector", lambda e, sview=sview, mview=mview, j=j, tt=tt: e.tensor_tensor(
                            mview[:, :, j], sview[:, :, j], tt, ALU.is_gt),
                            inc=sVS if j == 3 else None)
                    v8 = sVS.n
                    cmp_of[tq] = v8
                    p.wait("vector", sVS, v8)
                    if tq >= 1:
                        p.wait("vector", sHTg[grp], hs_tr[grp][tq - 1])
                    elif grp > 0:
                        p.wait("vector", sHTg[grp - 1], hs_tr[grp - 1][7])
                    elif half == 1:
                        p.wait("vector", sHTg[NGRP - 1], prev_half_hstr)
                    ha = p.run("vector", lambda e, hb_buf=hb_buf, mask=mask: e.tensor_mul(
                        hs_sb[:], hb_buf[:], mask), inc=sDVE)

                    ic0 = grp * 4
                    tens = T1 if ic0 < 32 else T2
                    oview = tens[:].rearrange("p (ic t) -> p ic t", ic=32)[
                        :, (ic0 % 32):(ic0 % 32) + 4, tq * P:(tq + 1) * P]
                    if half == 1 and grp == 0 and tq == 0:
                        p.wait("sync", sDVE, ln2_last_v)
                    p.wait("sync", sDVE, ha)
                    hs_tr[grp][tq] = p.run("sync", lambda e, oview=oview: e.dma_start(
                        out=oview, in_=hs_sb[:], transpose=True), inc=sHTg[grp], k=16)

                for tq in range(8):
                    ps = ps_g4[tq % 4]
                    if tq >= 4:
                        p.wait("tensor", sDVE, hb_done[grp][tq - 4])
                    else:
                        p.wait("tensor", sACT, silu_done[grp][4 + tq])
                    for kc in range(KC):
                        if tq == 0 and kc in (0, 8):
                            p.wait("tensor", sWUp[kc // 8], wu_done[grp][kc // 8])
                        p.run("tensor", lambda e, kc=kc, tq=tq, ps=ps: e.matmul(
                            ps[:], xap(kc, tq), wu_ap(kc),
                            start=(kc == 0), stop=(kc == KC - 1)),
                            inc=sPE if kc == KC - 1 else None)
                    ud = sPE.n
                    p.wait("vector", sPE, ud)
                    if tq >= 2:
                        p.wait("vector", sACT, sq_of[tq - 2])
                    elif grp > 0:
                        p.wait("vector", sACT, sq_done2[grp - 1][tq + 6])
                    elif half == 1 and tq == 0:
                        p.wait("vector", sACT, prev_half_lnsq)
                    hb = p.run("vector", lambda e, tq=tq, ps=ps, gs=gs_ap(grp): e.tensor_mul(
                        hbuf2[tq % 2][:], gs[:, tq * GRP:(tq + 1) * GRP], ps[:]), inc=sDVE)
                    hb_done[grp][tq] = hb
                    p.wait("scalar", sDVE, hb)
                    if tq >= 2:
                        p.wait("scalar", sVS, cmp_of[tq - 2])
                    elif grp > 0:
                        p.wait("scalar", sVS, cmp_done2[grp - 1][tq + 6])
                    sq_of[tq] = p.run("scalar", lambda e, tq=tq: e.activation(
                        hsq2[tq % 2][:], hbuf2[tq % 2][:], AF.Square), inc=sACT)
                    if "hb0" in dumps and half == 0 and grp == 0 and tq == 0:
                        p.wait("sync", sDVE, hb)
                        p.run("sync", lambda e: e.dma_start(out=dump_aps["hb0"], in_=hbuf2[0][:]), inc=sQ, k=16)
                    if tq >= 1:
                        sparsify(tq - 1)
                sparsify(7)
                umm_done[grp] = sPE.n
                if grp + 1 < NGRP:
                    p.wait("gpsimd", sPE, gmm_done[grp])
                    wl = p.run("gpsimd", lambda e, grp=grp: e.dma_start(
                        out=wgsb_v[:, 0:8, :], in_=wg_r[grp + 1][:, 0:8, :]), inc=sWGp[0], k=16)
                    wh = p.run("gpsimd", lambda e, grp=grp: e.dma_start(
                        out=wgsb_v[:, 8:16, :], in_=wg_r[grp + 1][:, 8:16, :]), inc=sWGp[1], k=16)
                    wg_done[grp + 1] = (wl, wh)

            prev_half_hb = hb_done[NGRP - 1][7]
            prev_half_hstr = hs_tr[NGRP - 1][7]

            if "hst0" in dumps and half == 0:
                p.wait("sync", sHTg[NGRP - 1], hs_tr[NGRP - 1][7])
                p.run("sync", lambda e: e.dma_start(out=dump_aps["hst0"], in_=hsT_ap(0)), inc=sQ, k=16)


            wd_done = [None] * KC
            dmm_done = [None] * KC
            devac_done = [None] * KC
            p.wait("gpsimd", sPE, gmm_done[NGRP - 1])
            wd_done[0] = p.run("gpsimd", lambda e: e.dma_start(
                out=wd_slot(0).rearrange("p (ic c) -> p ic c", ic=IC),
                in_=wd_r[0]), inc=sWDp[0], k=16)
            p.wait("gpsimd", sPE, umm_done[NGRP - 1])
            wd_done[1] = p.run("gpsimd", lambda e: e.dma_start(
                out=wd_slot(1).rearrange("p (ic c) -> p ic c", ic=IC),
                in_=wd_r[1]), inc=sWDp[1], k=16)

            for hc in range(KC):
                ps = ps_d[hc % 2]
                p.wait("tensor", sWDp[hc % 2], wd_done[hc])
                if hc >= 2:
                    p.wait("tensor", sACT, devac_done[hc - 2])
                for ic in range(IC):
                    if hc == 0 and ic % 4 == 0:
                        p.wait("tensor", sHTg[ic // 4], hs_tr[ic // 4][7])
                    for n in range(2):
                        p.run("tensor", lambda e, ic=ic, n=n, hc=hc, ps=ps: e.matmul(
                            ps[:, n * 512:(n + 1) * 512],
                            wd_slot(hc)[:, ic * P:(ic + 1) * P],
                            hsT_ap(ic, n * 512, 512),
                            start=(ic == 0), stop=(ic == IC - 1)),
                            inc=sPE if (ic == IC - 1 and n == 1) else None)
                dmm_done[hc] = sPE.n
                p.wait("scalar", sPE, dmm_done[hc])
                devac_done[hc] = p.run("scalar", lambda e, hc=hc, ps=ps: e.copy(
                    downT_ap(hc), ps[:]), inc=sACT)
                if hc + 2 < KC:
                    p.wait("gpsimd", sPE, dmm_done[hc])
                    wd_done[hc + 2] = p.run("gpsimd", lambda e, hc=hc: e.dma_start(
                        out=wd_slot(hc + 2).rearrange("p (ic c) -> p ic c", ic=IC),
                        in_=wd_r[hc + 2]), inc=sWDp[hc % 2], k=16)
            prev_half_dmm = dmm_done[KC - 1]

            dtm_tr = []
            p.wait("sync", sACT, devac_done[-1])
            for hc in range(KC):
                oview = T1[:, 0:8 * H].rearrange("p (tq h) -> p tq h", tq=8)[
                    :, :, hc * P:(hc + 1) * P]
                v = p.run("sync", lambda e, hc=hc, oview=oview: e.dma_start(
                    out=oview, in_=downT_ap(hc), transpose=True), inc=sQ, k=16)
                dtm_tr.append(v)
            if half == 0:
                p.wait("scalar", sXSP, x1t_spill)
                for hc in range(KC):
                    p.wait("scalar", sQ, dtm_tr[hc])
                    next_x1t_load = p.run("scalar", lambda e, hc=hc: e.dma_start(
                        out=T3x_v[:, hc:hc + 1, :],
                        in_=scr_x1t_v[:, hc:hc + 1, HALF:2 * HALF]), inc=sXL2, k=16)
            if "dtm0" in dumps and half == 0:
                p.wait("sync", sQ, dtm_tr[-1])
                p.run("sync", lambda e: e.dma_start(out=dump_aps["dtm0"], in_=dtm_ap(0)), inc=sQ, k=16)

            if half == 0:
                p.wait("gpsimd", sPE, dmm_done[KC - 1])
                wl = p.run("gpsimd", lambda e: e.dma_start(
                    out=wgsb_v[:, 0:8, :], in_=wg_r[0][:, 0:8, :]), inc=sWGp[0], k=16)
                wh = p.run("gpsimd", lambda e: e.dma_start(
                    out=wgsb_v[:, 8:16, :], in_=wg_r[0][:, 8:16, :]), inc=sWGp[1], k=16)
                next_wg0 = (wl, wh)

            osb = SCR[:, 0:4096].bitcast(F32)
            sdump = SCR[:, 4096:6144]

            r1_load = [None] * 8
            ost = [None] * 8
            rv_of = [None] * 8
            sq2_of = [None] * 8
            p.wait("sync", sPE, prev_half_dmm)
            for i in range(2):
                r1_load[i] = p.run("sync", lambda e, i=i, half=half: e.dma_start(
                    out=r1l2[i], in_=scr_rows[half * 8 + i]), inc=sRL[i % 2], k=16)

            def ln2_stats(tql):
                tqc = half * 8 + tql
                r2 = r2l2[tql % 2]
                b = (tql % 2) * 8
                p.wait("scalar", sDVE, rv_of[tql])
                cp = p.run("scalar", lambda e, r2=r2, b=b: e.activation(
                    sdump[:], r2, AF.Copy, accum_out=lncol[:, b:b + 1]), inc=sACT)
                sq = p.run("scalar", lambda e, r2=r2, b=b: e.activation(
                    sdump[:], r2, AF.Square, accum_out=lncol[:, b + 1:b + 2]), inc=sACT)
                sq2_of[tql] = sq
                p.wait("vector", sACT, cp)
                mv = p.run("vector", lambda e, b=b: e.tensor_scalar(
                    lncol[:, b + 5:b + 6], lncol[:, b:b + 1], 1.0 / H, None, ALU.mult), inc=sVS)
                p.wait("vector", sVS, mv)
                m2 = p.run("vector", lambda e, b=b: e.tensor_scalar(
                    lncol[:, b + 2:b + 3], lncol[:, b + 5:b + 6], lncol[:, b + 5:b + 6],
                    -EPS, ALU.mult, ALU.add), inc=sVS)
                p.wait("vector", sACT, sq)
                p.wait("vector", sVS, m2)
                vpe = p.run("vector", lambda e, b=b: e.tensor_scalar(
                    lncol[:, b + 3:b + 4], lncol[:, b + 1:b + 2], 1.0 / H,
                    lncol[:, b + 2:b + 3], ALU.mult, ALU.subtract), inc=sVS)
                p.wait("scalar", sVS, vpe)
                lnv = p.run("scalar", lambda e, b=b: e.activation(
                    lncol[:, b + 4:b + 5], lncol[:, b + 3:b + 4], AF.Ln), inc=sSS)
                p.wait("scalar", sSS, lnv)
                rs = p.run("scalar", lambda e, b=b: e.activation(
                    lncol[:, b + 6:b + 7], lncol[:, b + 4:b + 5], AF.Exp, scale=-0.5), inc=sACT)
                p.wait("vector", sACT, rs)
                if tql >= 1:
                    p.wait("vector", sQ, ost[tql - 1])
                ov = p.run("vector", lambda e, r2=r2, b=b: e.tensor_scalar(
                    osb[:], r2, lncol[:, b + 5:b + 6], lncol[:, b + 6:b + 7],
                    ALU.subtract, ALU.mult), inc=sDVE)
                p.wait("sync", sDVE, ov)
                ost[tql] = p.run("sync", lambda e, tqc=tqc: e.dma_start(
                    out=out_rows[tqc], in_=osb[:]), inc=sQ, k=16)
                return ov

            for tql in range(8):
                tqc = half * 8 + tql
                p.wait("vector", sRL[tql % 2], r1_load[tql])
                if tql == 0:
                    p.wait("vector", sQ, dtm_tr[-1])
                if tql >= 2:
                    p.wait("vector", sACT, sq2_of[tql - 2])
                rv = p.run("vector", lambda e, tql=tql, tqc=tqc: e.scalar_tensor_tensor(
                    r2l2[tql % 2], r1l2[tql % 2], rcols[:, tqc:tqc + 1], dtm_ap(tql),
                    ALU.mult, ALU.add), inc=sDVE)
                rv_of[tql] = rv
                if tql + 2 < 8:
                    p.wait("sync", sDVE, rv)
                    r1_load[tql + 2] = p.run("sync", lambda e, tql=tql, half=half: e.dma_start(
                        out=r1l2[tql % 2], in_=scr_rows[half * 8 + tql + 2]), inc=sRL[tql % 2], k=16)
                if tql >= 1:
                    ov = ln2_stats(tql - 1)
            ov = ln2_stats(7)
            ln2_last_v = ov
            ln2_last_store = ost[7]
            prev_half_lnsq = sq2_of[7]


        e_psum.close()

        for s in (sQ, *sHTg, sPC, sXSP, sXL2, sRL[0], sRL[1]):
            p.wait("sync", s, s.n)
        for s in (sG, sWV, *sXC, *sWB, *sXL, sSP, *sWGp, *sWUp, *sWDp):
            p.wait("gpsimd", s, s.n)

        block = top.enter_context(nc.Block())
        p.replay(block)

    return nc



def _make_tables():
    i = np.arange(32, dtype=np.float64)
    freq = ROPE_THETA ** (-i / 32.0)
    t = np.arange(T, dtype=np.float64)
    ang = np.outer(freq, t)
    c = np.cos(ang)
    s = np.sin(ang)
    ctab = np.ones((P, T), np.float32)
    stab = np.zeros((P, T), np.float32)
    for ii in range(32):
        ctab[2 * ii] = c[ii]
        ctab[2 * ii + 1] = c[ii]
        stab[2 * ii] = -s[ii]
        stab[2 * ii + 1] = s[ii]
    permm = np.zeros((P, P), ml_dtypes.bfloat16)
    for a in range(0, P, 2):
        permm[a, a + 1] = 1
        permm[a + 1, a] = 1
    onesm = np.ones((P, P), ml_dtypes.bfloat16)
    identm = np.eye(P).astype(ml_dtypes.bfloat16)
    return ctab.astype(ml_dtypes.bfloat16), stab.astype(ml_dtypes.bfloat16), permm, onesm, identm


def _prep_weights(Wq, Wk, Wg, Wu, Wd):
    wq_r = np.ascontiguousarray(Wq.reshape(KC, P, NH, DH).transpose(2, 1, 0, 3))
    wk_r = np.ascontiguousarray(Wk.reshape(KC, P, NKV, DH).transpose(2, 1, 0, 3))
    wg_r = np.ascontiguousarray(Wg.reshape(KC, P, NGRP, GRP).transpose(2, 1, 0, 3))
    wu_r = np.ascontiguousarray(Wu.reshape(KC, P, NGRP, GRP).transpose(2, 1, 0, 3))
    wd_r = np.ascontiguousarray(Wd.reshape(IC, P, KC, P).transpose(2, 1, 0, 3))
    return wq_r, wk_r, wg_r, wu_r, wd_r


def kernel(L0_attn_hidden_states, L0_attn_Wq, L0_attn_Wk, L0_attn_Wv, L0_attn_Wo,
           L0_ffn_w_gate, L0_ffn_w_up, L0_ffn_w_down):
    hs = np.asarray(L0_attn_hidden_states, np.float32)
    Wq = np.asarray(L0_attn_Wq, np.float32)
    Wk = np.asarray(L0_attn_Wk, np.float32)
    Wv = np.asarray(L0_attn_Wv, np.float32)
    Wo = np.asarray(L0_attn_Wo, np.float32)
    Wg = np.asarray(L0_ffn_w_gate, np.float32)
    Wu = np.asarray(L0_ffn_w_up, np.float32)
    Wd = np.asarray(L0_ffn_w_down, np.float32)

    ctab, stab, permm, onesm, identm = _make_tables()
    wq_r, wk_r, wg_r, wu_r, wd_r = _prep_weights(Wq, Wk, Wg, Wu, Wd)

    nc = build_core_program()
    base = {
        "wq_r": wq_r, "wk_r": wk_r, "wv": np.ascontiguousarray(Wv),
        "wo": np.ascontiguousarray(Wo),
        "wg_r": wg_r, "wu_r": wu_r, "wd_r": wd_r,
        "ctab": ctab, "stab": stab, "perm": permm, "onesm": onesm, "ident": identm,
    }
    in_maps = []
    for core in range(8):
        m = dict(base)
        m["x"] = np.ascontiguousarray(hs[core])
        in_maps.append(m)
    res = run_bass_kernel_spmd(nc, in_maps, core_ids=list(range(8)))
    out = np.stack([r["out"] for r in res.results], axis=0)
    return out.astype(np.float32)


# version 15 (final)
# speedup vs baseline: 1.1116x; 1.0002x over previous
import math
from contextlib import ExitStack

import numpy as np
import ml_dtypes

import concourse.bass as bass
import concourse.mybir as mybir
from concourse.bass_utils import run_bass_kernel_spmd

F32 = mybir.dt.float32
BF16 = mybir.dt.bfloat16
AF = mybir.ActivationFunctionType
ALU = mybir.AluOpType
AX = mybir.AxisListType

P = 128
T = 2048
H = 2048
TC = T // P
KC = H // P
NH, NKV, DH = 16, 4, 128
I = 8192
IC = I // P
GRP = 512
NGRP = I // GRP
HALF = 1024
ROPE_THETA = 10000.0
SCALE = 1.0 / math.sqrt(DH)
EPS = 1e-5

ENGINES = ("sync", "gpsimd", "tensor", "scalar", "vector")


class Sem:
    def __init__(self, handle):
        self.h = handle
        self.n = 0

    def bump(self, k):
        self.n += k
        return self.n


class Prog:

    def __init__(self):
        self.streams = {e: [] for e in ENGINES}

    def wait(self, eng, sem, val):
        if val is None:
            return
        self.streams[eng].append(lambda e, s=sem.h, v=val: e.wait_ge(s, v))

    def run(self, eng, fn, inc=None, k=1):
        if inc is None:
            self.streams[eng].append(fn)
            return None
        v = inc.bump(k)
        self.streams[eng].append(lambda e, f=fn, s=inc.h, kk=k: f(e).then_inc(s, kk))
        return v

    def replay(self, block):
        for name in ENGINES:
            stream = self.streams[name]

            def make(st):
                def body(eng):
                    for fn in st:
                        fn(eng)
                return body
            getattr(block, name)(make(stream))


def build_core_program(dumps=()):
    dumps = set(dumps)
    nc = bass.Bass("TRN2", target_bir_lowering=False, detect_race_conditions=False)

    x_in = nc.dram_tensor("x", [T, H], F32, kind="ExternalInput").ap()
    wq_r = nc.dram_tensor("wq_r", [NH, P, KC, DH], F32, kind="ExternalInput").ap()
    wk_r = nc.dram_tensor("wk_r", [NKV, P, KC, DH], F32, kind="ExternalInput").ap()
    wv_in = nc.dram_tensor("wv", [H, NKV * DH], F32, kind="ExternalInput").ap()
    wo_in = nc.dram_tensor("wo", [H, H], F32, kind="ExternalInput").ap()
    wg_r = nc.dram_tensor("wg_r", [NGRP, P, KC, GRP], F32, kind="ExternalInput").ap()
    wu_r = nc.dram_tensor("wu_r", [NGRP, P, KC, GRP], F32, kind="ExternalInput").ap()
    wd_r = nc.dram_tensor("wd_r", [KC, P, IC, P], F32, kind="ExternalInput").ap()
    ctab_in = nc.dram_tensor("ctab", [P, T], BF16, kind="ExternalInput").ap()
    stab_in = nc.dram_tensor("stab", [P, T], BF16, kind="ExternalInput").ap()
    perm_in = nc.dram_tensor("perm", [P, P], BF16, kind="ExternalInput").ap()
    ones_in = nc.dram_tensor("onesm", [P, P], BF16, kind="ExternalInput").ap()
    ident_in = nc.dram_tensor("ident", [P, P], BF16, kind="ExternalInput").ap()
    out_hbm = nc.dram_tensor("out", [T, H], F32, kind="ExternalOutput").ap()
    scr_r1 = nc.dram_tensor("scr_r1", [T, H], BF16, kind="Internal").ap()
    scr_x1t = nc.dram_tensor("scr_x1t", [P, KC * T], BF16, kind="Internal").ap()

    dump_aps = {}
    _dump_specs = {
        "xT": ([P, KC * T], BF16), "q0": ([P, T], BF16), "k0": ([P, T], BF16),
        "vall": ([P, TC * 512], BF16), "oallt": ([P, NH * T], BF16),
        "z0": ([P, 1024], F32), "x1bf": ([P, TC * H], BF16),
        "x1t": ([P, KC * T], BF16), "hst0": ([P, HALF], BF16), "dtm0": ([P, H], BF16),
        "gs0": ([P, GRP], BF16), "hb0": ([P, GRP], BF16),
    }
    for tag in dumps:
        shp, dt = _dump_specs[tag]
        dump_aps[tag] = nc.dram_tensor("dbg_" + tag, shp, dt, kind="ExternalOutput").ap()

    p = Prog()

    with ExitStack() as top:
        def sem(name):
            return Sem(top.enter_context(nc.semaphore(name)))

        T1 = top.enter_context(nc.sbuf_tensor("T1", [P, 32768], BF16))
        T2 = top.enter_context(nc.sbuf_tensor("T2", [P, 32768], BF16))
        T3 = top.enter_context(nc.sbuf_tensor("T3", [P, 32768], BF16))

        perm_sb = top.enter_context(nc.sbuf_tensor("perm_sb", [P, P], BF16))
        ones_sb = top.enter_context(nc.sbuf_tensor("ones_sb", [P, P], BF16))
        ident_sb = top.enter_context(nc.sbuf_tensor("ident_sb", [P, P], BF16))
        mcols = top.enter_context(nc.sbuf_tensor("mcols", [P, TC], F32))
        rcols = top.enter_context(nc.sbuf_tensor("rcols", [P, TC], F32))
        lncol = top.enter_context(nc.sbuf_tensor("lncol", [P, 16], F32))
        SCR = top.enter_context(nc.sbuf_tensor("SCR", [P, 7552], BF16))

        sQ = sem("sQ")
        sG = sem("sG")
        sPE = sem("sPE")
        sACT = sem("sACT")
        sDVE = sem("sDVE")
        sVS = sem("sVS")
        sSS = sem("sSS")
        sWV = sem("sWV")
        sXC = [sem(f"sXC{i}") for i in range(4)]
        sWB = [sem(f"sWB{i}") for i in range(2)]
        sXL = [sem(f"sXL{i}") for i in range(8)]
        sSP = sem("sSP")
        sWGp = [sem(f"sWG{i}") for i in range(2)]
        sWUp = [sem(f"sWU{i}") for i in range(2)]
        sWDp = [sem(f"sWD{i}") for i in range(2)]
        sHTg = [sem(f"sHTg{i}") for i in range(NGRP)]
        sPC = sem("sPC")
        sXSP = sem("sXSP")
        sXL2 = sem("sXTL")
        sRL = [sem(f"sRL{i}") for i in range(2)]

        def t_slot(tens, slot):
            return tens[:, slot * T:(slot + 1) * T]

        K_OFF, V_OFF, QSB_OFF = 0, 8192, 16384
        T1R_OFF, T2R_OFF, WDB_OFF, CT_OFF, ST_OFF = 18432, 20480, 22528, 26624, 28672
        EXP_OFF = QSB_OFF

        def kslot(g):
            return T3[:, K_OFF + g * T: K_OFF + (g + 1) * T]

        def vslot(tkc):
            return T3[:, V_OFF + tkc * 512: V_OFF + (tkc + 1) * 512]

        qsb_ap = T3[:, QSB_OFF:QSB_OFF + T]
        t1_ap = T3[:, T1R_OFF:T1R_OFF + T]
        t2_ap = T3[:, T2R_OFF:T2R_OFF + T]
        ctab_sb = T3[:, CT_OFF:CT_OFF + T]
        stab_sb = T3[:, ST_OFF:ST_OFF + T]

        def wdbslot(i):
            o = WDB_OFF + (i % 2) * 2048
            return T3[:, o:o + 2048]

        def expslot(i):
            o = EXP_OFF + (i % 3) * 1024
            return T3[:, o:o + 1024]

        X1T_OFF, WG_OFF, WU_OFF = 0, 16384, 24576

        def x1t_ap(kc, lo, n):
            o = X1T_OFF + kc * HALF + lo
            return T3[:, o:o + n]

        def wg_ap(kc):
            return T3[:, WG_OFF + kc * GRP: WG_OFF + (kc + 1) * GRP]

        def wu_ap(kc):
            return T3[:, WU_OFF + kc * GRP: WU_OFF + (kc + 1) * GRP]

        def wd_slot(i):
            o = WG_OFF + (i % 2) * 8192
            return T3[:, o:o + 8192]

        def hsT_ap(ic, lo=0, n=HALF):
            tens = T1 if ic < 32 else T2
            o = (ic % 32) * HALF + lo
            return tens[:, o:o + n]

        def downT_ap(hc, lo=0, n=HALF):
            return T3[:, hc * HALF + lo: hc * HALF + lo + n]

        def dtm_ap(tql, lo=0, n=H):
            return T1[:, tql * H + lo: tql * H + lo + n]

        r1l2 = [T2[:, 0:H], T2[:, H:2 * H]]
        r2l2 = [T2[:, 2 * H:3 * H], T2[:, 3 * H:4 * H]]

        xin_v = x_in.rearrange("(tc p) h -> p tc h", p=P)
        xbf_v = T2[:].rearrange("p (tc h) -> p tc h", tc=TC)
        wv_v = wv_in.rearrange("(kc p) c -> p kc c", p=P)
        wvsb_v = T3[:, 0:8192].rearrange("p (kc c) -> p kc c", kc=KC)
        wv_done = p.run("gpsimd", lambda e: e.dma_start(out=wvsb_v, in_=wv_v), inc=sWV, k=16)

        xcast_done = [None] * 4
        for i in range(4):
            xcast_done[i] = p.run("gpsimd", lambda e, i=i: e.dma_start(
                out=xbf_v[:, i * 4:(i + 1) * 4, :], in_=xin_v[:, i * 4:(i + 1) * 4, :]),
                inc=sXC[i], k=16)

        p.run("sync", lambda e: e.dma_start(out=ctab_sb, in_=ctab_in), inc=sQ, k=16)
        p.run("sync", lambda e: e.dma_start(out=stab_sb, in_=stab_in), inc=sQ, k=16)
        p.run("sync", lambda e: e.dma_start(out=perm_sb[:], in_=perm_in), inc=sQ, k=16)
        p.run("sync", lambda e: e.dma_start(out=ones_sb[:], in_=ones_in), inc=sQ, k=16)
        tabs_done = p.run("sync", lambda e: e.dma_start(out=ident_sb[:], in_=ident_in), inc=sQ, k=16)

        T1_hview = T1[:].rearrange("p (hc t) -> p hc t", hc=KC)
        a_psum = ExitStack()
        a_ps = [a_psum.enter_context(nc.psum_tensor(f"a_ps{i}", [P, H], BF16)) for i in range(2)]
        xT_done = []
        prev_evc = [None, None]
        p.wait("tensor", sQ, tabs_done)
        for tc in range(TC):
            aps = a_ps[tc % 2]
            p.wait("tensor", sXC[tc // 4], xcast_done[tc // 4])
            p.wait("tensor", sACT, prev_evc[tc % 2])
            for hc in range(KC):
                p.run("tensor", lambda e, tc=tc, hc=hc, aps=aps: e.transpose(
                    aps[:, hc * P:(hc + 1) * P],
                    T2[:, tc * H + hc * P: tc * H + (hc + 1) * P], ident_sb[:]),
                    inc=sPE if hc == KC - 1 else None)
            trd = sPE.n
            p.wait("scalar", sPE, trd)
            v = p.run("scalar", lambda e, tc=tc, aps=aps: e.copy(
                T1_hview[:, :, tc * P:(tc + 1) * P], aps[:]), inc=sACT)
            prev_evc[tc % 2] = v
            xT_done.append(v)

        if "xT" in dumps:
            p.wait("sync", sACT, xT_done[-1])
            p.run("sync", lambda e: e.dma_start(out=dump_aps["xT"], in_=T1[:]), inc=sQ, k=16)

        a_psum.close()
        b_psum = ExitStack()
        ps_bq = b_psum.enter_context(nc.psum_tensor("ps_bq", [P, T], F32))
        ps_bp = b_psum.enter_context(nc.psum_tensor("ps_bp", [P, T], F32))

        p.wait("tensor", sACT, xT_done[-1])
        p.wait("tensor", sWV, wv_done)
        v_evac = [None] * TC
        for tkc in range(TC):
            sl = ps_bq[:, (tkc % 4) * 512:(tkc % 4) * 512 + 512]
            if tkc >= 4:
                p.wait("tensor", sACT, v_evac[tkc - 4])
            for kc in range(KC):
                p.run("tensor", lambda e, kc=kc, tkc=tkc, sl=sl: e.matmul(
                    sl, T1[:, kc * T + tkc * P: kc * T + (tkc + 1) * P],
                    wvsb_v[:, kc, :], start=(kc == 0), stop=(kc == KC - 1)),
                    inc=sPE if kc == KC - 1 else None)
            mmv = sPE.n
            p.wait("scalar", sPE, mmv)
            v_evac[tkc] = p.run("scalar", lambda e, tkc=tkc, sl=sl: e.copy(vslot(tkc), sl), inc=sACT)

        if "vall" in dumps:
            p.wait("sync", sACT, v_evac[-1])
            p.run("sync", lambda e: e.dma_start(
                out=dump_aps["vall"], in_=T3[:, V_OFF:V_OFF + TC * 512]), inc=sQ, k=16)

        head_rope_done = [None] * NH
        kv_rope_done = [None] * NKV
        trk = {
            "prev_wmm": [None, None],
            "prev_evac": v_evac[TC - 1],
            "prev_t2": None,
        }

        def emit_proj_rope(idx, wr_dram, dst_ap, done_list):
            wsl = wdbslot(idx)
            wsb_v = wsl.rearrange("p (kc c) -> p kc c", kc=KC)
            win_v = wr_dram[idx]
            if trk["prev_wmm"][idx % 2] is not None:
                p.wait("gpsimd", sPE, trk["prev_wmm"][idx % 2])
            wdone = p.run("gpsimd", lambda e, wsb_v=wsb_v, win_v=win_v: e.dma_start(
                out=wsb_v, in_=win_v), inc=sWB[idx % 2], k=16)

            p.wait("tensor", sWB[idx % 2], wdone)
            p.wait("tensor", sACT, trk["prev_evac"])
            for kc in range(KC):
                for n in range(4):
                    p.run("tensor", lambda e, kc=kc, n=n, wsb_v=wsb_v: e.matmul(
                        ps_bq[:, n * 512:(n + 1) * 512], wsb_v[:, kc, :],
                        T1[:, kc * T + n * 512: kc * T + (n + 1) * 512],
                        start=(kc == 0), stop=(kc == KC - 1)),
                        inc=sPE if (kc == KC - 1 and n == 3) else None)
            mm_done = sPE.n
            trk["prev_wmm"][idx % 2] = mm_done

            p.wait("scalar", sPE, mm_done)
            p.wait("scalar", sVS, trk["prev_t2"])
            evac = p.run("scalar", lambda e: e.copy(qsb_ap, ps_bq[:]), inc=sACT)
            trk["prev_evac"] = evac

            p.wait("tensor", sACT, evac)
            p.wait("tensor", sVS, trk["prev_t2"])
            for n in range(4):
                p.run("tensor", lambda e, n=n: e.matmul(
                    ps_bp[:, n * 512:(n + 1) * 512], perm_sb[:],
                    qsb_ap[:, n * 512:(n + 1) * 512], start=True, stop=True),
                    inc=sPE if n == 3 else None)
            pm_done = sPE.n

            p.wait("vector", sACT, evac)
            p.run("vector", lambda e: e.tensor_mul(t1_ap, qsb_ap, ctab_sb))
            p.wait("vector", sPE, pm_done)
            t2v = p.run("vector", lambda e: e.tensor_mul(t2_ap, ps_bp[:], stab_sb), inc=sVS)
            trk["prev_t2"] = t2v
            p.wait("vector", sVS, t2v)
            done_list[idx] = p.run("vector", lambda e, d=dst_ap: e.tensor_add(d, t1_ap, t2_ap), inc=sDVE)

        p.wait("tensor", sQ, tabs_done)
        p.wait("vector", sQ, tabs_done)
        for g in range(NKV):
            emit_proj_rope(g, wk_r, kslot(g), kv_rope_done)
        for h in range(NH):
            emit_proj_rope(h, wq_r, t_slot(T2, h), head_rope_done)

        if "q0" in dumps:
            p.wait("sync", sDVE, head_rope_done[0])
            p.run("sync", lambda e: e.dma_start(out=dump_aps["q0"], in_=t_slot(T2, 0)), inc=sQ, k=16)
        if "k0" in dumps:
            p.wait("sync", sDVE, kv_rope_done[0])
            p.run("sync", lambda e: e.dma_start(out=dump_aps["k0"], in_=kslot(0)), inc=sQ, k=16)

        b_psum.close()

        xin_rows = x_in.rearrange("(tc p) h -> tc p h", p=P)
        xload_done = [None] * TC
        p.wait("gpsimd", sDVE, head_rope_done[NH - 1])
        for tqc in range(10, TC):
            xload_done[tqc] = p.run("gpsimd", lambda e, tqc=tqc: e.dma_start(
                out=t_slot(T3, tqc), in_=xin_rows[tqc]), inc=sXL[tqc % 8], k=16)

        c_psum = ExitStack()
        ps_sc = c_psum.enter_context(nc.psum_tensor("ps_sc", [P, 3072], F32))
        ps_pv = c_psum.enter_context(nc.psum_tensor("ps_pv", [P, 1024], F32))
        zk = [SCR[:, i * 1024:(i + 1) * 1024] for i in range(2)]
        zrec = SCR[:, 2048:4096].bitcast(F32)

        p.wait("tensor", sVS, trk["prev_t2"])
        p.wait("scalar", sVS, trk["prev_t2"])
        p.wait("tensor", sDVE, kv_rope_done[NKV - 1])

        head_done = [None] * NH
        last_exp3 = [None, None, None]
        last_z3 = [None, None, None]
        prev_pv_evac = None
        prev_zrec = None
        last_zrv = None
        wo_loaded_h = [None] * NH
        for h in range(NH):
            g = h // NKV
            p.wait("tensor", sDVE, head_rope_done[h])
            for slab in range(2):
                qslab = t_slot(T2, h)[:, slab * 1024:(slab + 1) * 1024]
                exp_of = {}
                zadd_of = {}

                def scores(c):
                    third = c % 3
                    p.wait("tensor", sACT, last_exp3[third])
                    if third == 2:
                        p.wait("tensor", sSS, prev_zrec)
                    for n in range(2):
                        p.run("tensor", lambda e, c=c, n=n, third=third, qslab=qslab, g=g: e.matmul(
                            ps_sc[:, third * 1024 + n * 512: third * 1024 + (n + 1) * 512],
                            kslot(g)[:, c * P:(c + 1) * P],
                            qslab[:, n * 512:(n + 1) * 512], start=True, stop=True),
                            inc=sPE if n == 1 else None)
                    sc_done = sPE.n
                    p.wait("scalar", sPE, sc_done)
                    p.wait("scalar", sVS, last_z3[third])
                    ev = p.run("scalar", lambda e, c=c, third=third: e.activation(
                        expslot(c), ps_sc[:, third * 1024:(third + 1) * 1024],
                        AF.Exp, scale=SCALE), inc=sACT)
                    last_exp3[third] = ev
                    exp_of[c] = ev
                    p.wait("vector", sACT, ev)
                    if c == 0:
                        zv = p.run("vector", lambda e, c=c: e.tensor_copy(
                            zk[0][:], expslot(c)), inc=sVS)
                    else:
                        p.wait("vector", sVS, zadd_of[c - 1])
                        zv = p.run("vector", lambda e, c=c: e.tensor_add(
                            zk[c % 2][:], zk[(c - 1) % 2][:], expslot(c)), inc=sVS)
                    last_z3[third] = zv
                    zadd_of[c] = zv

                def pv(c):
                    p.wait("tensor", sACT, exp_of[c])
                    if c == 0 and prev_pv_evac is not None:
                        p.wait("tensor", sDVE, prev_pv_evac)
                    for n in range(2):
                        p.run("tensor", lambda e, c=c, n=n, g=g: e.matmul(
                            ps_pv[:, n * 512:(n + 1) * 512],
                            vslot(c)[:, g * P:(g + 1) * P],
                            expslot(c)[:, n * 512:(n + 1) * 512],
                            start=(c == 0), stop=(c == TC - 1)),
                            inc=sPE if (n == 1 and c == TC - 1) else None)

                scores(0)
                scores(1)
                scores(2)
                for c in range(3, TC):
                    pv(c - 3)
                    scores(c)
                pv(TC - 3)
                pv(TC - 2)
                p.wait("tensor", sVS, zadd_of[TC - 1])
                for n in range(2):
                    p.run("tensor", lambda e, n=n: e.matmul(
                        ps_sc[:, 2048 + n * 512: 2048 + (n + 1) * 512], ones_sb[:],
                        zk[(TC - 1) % 2][:, n * 512:(n + 1) * 512],
                        start=True, stop=True), inc=sPE if n == 1 else None)
                zmm_done = sPE.n
                pv(TC - 1)
                pv_done = sPE.n

                p.wait("scalar", sPE, zmm_done)
                p.wait("scalar", sDVE, prev_pv_evac)
                lnz = p.run("scalar", lambda e: e.activation(
                    zrec[:], ps_sc[:, 2048:3072], AF.Ln), inc=sSS)
                p.wait("scalar", sSS, lnz)
                zrv = p.run("scalar", lambda e: e.activation(
                    zrec[:], zrec[:], AF.Exp, scale=-1.0), inc=sACT)
                last_zrv = zrv
                p.wait("vector", sPE, pv_done)
                p.wait("vector", sACT, zrv)
                dst = t_slot(T1, h)[:, slab * 1024:(slab + 1) * 1024]
                prev_pv_evac = p.run("vector", lambda e, dst=dst: e.tensor_mul(
                    dst, ps_pv[:], zrec[:]), inc=sDVE)
                prev_zrec = lnz
                if "z0" in dumps and h == 0 and slab == 0:
                    p.wait("sync", sACT, zrv)
                    p.run("sync", lambda e: e.dma_start(
                        out=dump_aps["z0"], in_=zrec[:]), inc=sQ, k=16)
            head_done[h] = prev_pv_evac

            wo_v = wo_in.rearrange("(kc p) c -> p kc c", p=P)[:, h, :]
            p.wait("gpsimd", sDVE, head_done[h])
            wo_loaded_h[h] = p.run("gpsimd", lambda e, h=h, wo_v=wo_v: e.dma_start(
                out=t_slot(T2, h), in_=wo_v), inc=sG, k=16)

        if "oallt" in dumps:
            p.wait("sync", sDVE, head_done[-1])
            p.run("sync", lambda e: e.dma_start(out=dump_aps["oallt"], in_=T1[:]), inc=sQ, k=16)

        c_psum.close()

        d_psum = ExitStack()
        ps_wo = [d_psum.enter_context(nc.psum_tensor(f"ps_wo{i}", [P, H], F32)) for i in range(2)]
        r1_sb = SCR[:, 0:4096].bitcast(F32)

        scr_rows = scr_r1.rearrange("(tc p) h -> tc p h", p=P)
        out_rows = out_hbm.rearrange("(tc p) h -> tc p h", p=P)

        for tqc in range(8):
            xload_done[tqc] = p.run("gpsimd", lambda e, tqc=tqc: e.dma_start(
                out=t_slot(T3, tqc), in_=xin_rows[tqc]), inc=sXL[tqc % 8], k=16)

        r1_done = [None] * TC
        r1_spill = [None] * TC
        sq_done = [None] * TC
        x1bf_done = [None] * TC
        womm_done = [None] * TC
        for tqc in range(TC):
            ps = ps_wo[tqc % 2]
            if tqc >= 2:
                p.wait("tensor", sDVE, r1_done[tqc - 2])
            elif tqc == 1:
                p.wait("tensor", sACT, last_zrv)
                p.wait("tensor", sDVE, head_done[NH - 1])
            for kc in range(KC):
                if tqc == 0:
                    p.wait("tensor", sG, wo_loaded_h[kc])
                for n in range(4):
                    p.run("tensor", lambda e, kc=kc, n=n, tqc=tqc, ps=ps: e.matmul(
                        ps[:, n * 512:(n + 1) * 512],
                        t_slot(T1, kc)[:, tqc * P:(tqc + 1) * P],
                        t_slot(T2, kc)[:, n * 512:(n + 1) * 512],
                        start=(kc == 0), stop=(kc == KC - 1)),
                        inc=sPE if (kc == KC - 1 and n == 3) else None)
            wo_done = sPE.n
            womm_done[tqc] = wo_done

            p.wait("vector", sXL[tqc % 8], xload_done[tqc])
            p.wait("vector", sPE, wo_done)
            if tqc >= 1:
                p.wait("vector", sSP, r1_spill[tqc - 1])
                p.wait("vector", sACT, sq_done[tqc - 1])
            rd = p.run("vector", lambda e, ps=ps, tqc=tqc: e.tensor_add(
                r1_sb[:], t_slot(T3, tqc), ps[:]), inc=sDVE)
            r1_done[tqc] = rd

            p.wait("gpsimd", sDVE, rd)
            r1_spill[tqc] = p.run("gpsimd", lambda e, tqc=tqc: e.dma_start(
                out=scr_rows[tqc], in_=r1_sb[:]), inc=sSP, k=16)
            if tqc + 8 < 10:
                xload_done[tqc + 8] = p.run("gpsimd", lambda e, t=tqc + 8: e.dma_start(
                    out=t_slot(T3, t), in_=xin_rows[t]), inc=sXL[tqc % 8], k=16)

            p.wait("scalar", sDVE, rd)
            cp = p.run("scalar", lambda e, tqc=tqc: e.activation(
                t_slot(T3, tqc), r1_sb[:], AF.Copy, accum_out=lncol[:, 1:2]), inc=sACT)
            sq = p.run("scalar", lambda e, tqc=tqc: e.activation(
                t_slot(T3, tqc), r1_sb[:], AF.Square, accum_out=lncol[:, 0:1]), inc=sACT)
            sq_done[tqc] = sq
            p.wait("vector", sACT, cp)
            mv = p.run("vector", lambda e, tqc=tqc: e.tensor_scalar(
                mcols[:, tqc:tqc + 1], lncol[:, 1:2], 1.0 / H, None, ALU.mult), inc=sVS)
            p.wait("vector", sVS, mv)
            m2 = p.run("vector", lambda e, tqc=tqc: e.tensor_scalar(
                lncol[:, 2:3], mcols[:, tqc:tqc + 1], mcols[:, tqc:tqc + 1], -EPS,
                ALU.mult, ALU.add), inc=sVS)
            p.wait("vector", sACT, sq)
            p.wait("vector", sVS, m2)
            vpe = p.run("vector", lambda e: e.tensor_scalar(
                lncol[:, 3:4], lncol[:, 0:1], 1.0 / H, lncol[:, 2:3],
                ALU.mult, ALU.subtract), inc=sVS)
            p.wait("scalar", sVS, vpe)
            lnv = p.run("scalar", lambda e: e.activation(lncol[:, 4:5], lncol[:, 3:4], AF.Ln), inc=sSS)
            p.wait("scalar", sSS, lnv)
            rs = p.run("scalar", lambda e, tqc=tqc: e.activation(
                rcols[:, tqc:tqc + 1], lncol[:, 4:5], AF.Exp, scale=-0.5), inc=sACT)
            p.wait("vector", sACT, rs)
            x1bf_done[tqc] = p.run("vector", lambda e, tqc=tqc: e.tensor_scalar(
                t_slot(T3, tqc), r1_sb[:], mcols[:, tqc:tqc + 1], rcols[:, tqc:tqc + 1],
                ALU.subtract, ALU.mult), inc=sDVE)

        if "x1bf" in dumps:
            p.wait("sync", sDVE, x1bf_done[-1])
            p.run("sync", lambda e: e.dma_start(out=dump_aps["x1bf"], in_=T3[:]), inc=sQ, k=16)

        d_psum.close()

        T2_hview = T2[:].rearrange("p (kc t) -> p kc t", kc=KC)
        dx_stack = ExitStack()
        dx_pad = dx_stack.enter_context(nc.psum_tensor("dx_pad", [P, H], F32))
        dx_ps = [dx_stack.enter_context(nc.psum_tensor(f"dx_ps{i}", [P, H], BF16)) for i in range(2)]
        x1t_evac = [None] * TC
        p.wait("tensor", sDVE, r1_done[15])
        for tc in range(TC):
            dps = dx_ps[tc % 2]
            p.wait("tensor", sDVE, x1bf_done[tc])
            p.wait("tensor", sACT, x1t_evac[tc - 2] if tc >= 2 else None)
            for hc in range(KC):
                p.run("tensor", lambda e, tc=tc, hc=hc, dps=dps: e.transpose(
                    dps[:, hc * P:(hc + 1) * P],
                    T3[:, tc * H + hc * P: tc * H + (hc + 1) * P], ident_sb[:]),
                    inc=sPE if hc == KC - 1 else None)
            trd = sPE.n
            p.wait("scalar", sPE, trd)
            x1t_evac[tc] = p.run("scalar", lambda e, tc=tc, dps=dps: e.copy(
                T2_hview[:, :, tc * P:(tc + 1) * P], dps[:]), inc=sACT)
        wgfree = x1t_evac[11]
        wufree = x1t_evac[15]
        x1t_pieces = []
        T3x_v = T3[:, 0:KC * HALF].rearrange("p (kc t) -> p kc t", kc=KC)
        p.wait("sync", sACT, x1t_evac[7])
        for blk in range(2):
            v = p.run("sync", lambda e, blk=blk: e.dma_start(
                out=T3x_v[:, :, blk * 512:(blk + 1) * 512],
                in_=T2_hview[:, :, blk * 512:(blk + 1) * 512]), inc=sPC, k=16)
            x1t_pieces.append(v)
        p.wait("sync", sACT, x1t_evac[15])
        x1t_spill = p.run("sync", lambda e: e.dma_start(out=scr_x1t, in_=T2[:]), inc=sXSP, k=16)
        if "x1t" in dumps:
            p.run("sync", lambda e: e.dma_start(out=dump_aps["x1t"], in_=T2[:]), inc=sQ, k=16)
        dx_stack.close()

        e_psum = ExitStack()
        ps_g4 = [e_psum.enter_context(nc.psum_tensor(f"ps_g{i}", [P, GRP], F32)) for i in range(4)]
        ps_d = [e_psum.enter_context(nc.psum_tensor(f"ps_d{i}", [P, HALF], F32)) for i in range(2)]

        scr_x1t_v = scr_x1t.rearrange("p (kc t) -> p kc t", kc=KC)
        wgsb_v = T3[:, WG_OFF:WG_OFF + KC * GRP].rearrange("p (kc c) -> p kc c", kc=KC)
        wusb_v = T3[:, WU_OFF:WU_OFF + KC * GRP].rearrange("p (kc c) -> p kc c", kc=KC)

        prev_half_dmm = None
        prev_half_hb = None
        prev_half_hstr = None
        prev_half_lnsq = None
        ln2_last_v = None
        ln2_last_store = None
        next_x1t_load = None
        next_wg0 = None
        for half in range(2):
            x1t_load = None if half == 0 else next_x1t_load

            gsilu = SCR[:, 0:8 * GRP]
            gsilu_g0 = T2[:, 4096:8192]

            def gs_ap(grp):
                return gsilu_g0 if (half == 1 and grp == 0) else gsilu
            hbuf2 = [SCR[:, 4096 + i * GRP: 4096 + (i + 1) * GRP] for i in range(2)]
            hsq2 = [SCR[:, 5120 + i * GRP: 5120 + (i + 1) * GRP] for i in range(2)]
            hs_sb = SCR[:, 6144:6144 + GRP]
            mmx = SCR[:, 6656:6656 + 7 * (GRP // 4)]

            wg_done = [None] * NGRP
            wu_done = [None] * NGRP
            gmm_done = [None] * NGRP
            umm_done = [None] * NGRP
            silu_done = [[None] * 8 for _ in range(NGRP)]
            hb_done = [[None] * 8 for _ in range(NGRP)]
            hs_tr = [[None] * 8 for _ in range(NGRP)]
            sq_done2 = [[None] * 8 for _ in range(NGRP)]
            cmp_done2 = [[None] * 8 for _ in range(NGRP)]

            if half == 0:
                p.wait("gpsimd", sACT, wgfree)
                wl = p.run("gpsimd", lambda e: e.dma_start(
                    out=wgsb_v[:, 0:8, :], in_=wg_r[0][:, 0:8, :]), inc=sWGp[0], k=16)
                wh = p.run("gpsimd", lambda e: e.dma_start(
                    out=wgsb_v[:, 8:16, :], in_=wg_r[0][:, 8:16, :]), inc=sWGp[1], k=16)
                wg_done[0] = (wl, wh)
            else:
                wg_done[0] = next_wg0

            for grp in range(NGRP):
                if half == 0 and grp < 8:
                    xap = lambda kc, tq: T2[:, kc * T + tq * P: kc * T + tq * P + P]
                else:
                    xap = lambda kc, tq: x1t_ap(kc, tq * P, P)
                for tq in range(8):
                    if grp == 0:
                        if half == 0:
                            p.wait("tensor", sACT, x1t_evac[tq])
                            p.wait("tensor", sPC, x1t_pieces[-1])
                        elif tq == 0:
                            p.wait("tensor", sXL2, x1t_load)
                    if half == 0 and grp == 8 and tq == 0:
                        p.wait("tensor", sPC, x1t_pieces[-1])
                    ps = ps_g4[tq % 4]
                    if tq >= 4:
                        p.wait("tensor", sACT, silu_done[grp][tq - 4])
                    elif grp > 0:
                        p.wait("tensor", sDVE, hb_done[grp - 1][tq + 4])
                    elif half == 1:
                        p.wait("tensor", sDVE, prev_half_hb)
                    for kc in range(KC):
                        if tq == 0 and kc in (0, 8):
                            p.wait("tensor", sWGp[kc // 8], wg_done[grp][kc // 8])
                        p.run("tensor", lambda e, kc=kc, tq=tq, ps=ps: e.matmul(
                            ps[:], xap(kc, tq), wg_ap(kc),
                            start=(kc == 0), stop=(kc == KC - 1)),
                            inc=sPE if kc == KC - 1 else None)
                    gd = sPE.n
                    p.wait("scalar", sPE, gd)
                    if grp > 0:
                        p.wait("scalar", sDVE, hb_done[grp - 1][tq])
                        if half == 1 and grp == 1 and tq == 0:
                            p.wait("scalar", sQ, ln2_last_store)
                    elif half == 0:
                        p.wait("scalar", sDVE, x1bf_done[TC - 1])
                        p.wait("scalar", sSP, r1_spill[TC - 1])
                    silu_done[grp][tq] = p.run("scalar", lambda e, tq=tq, ps=ps, gs=gs_ap(grp): e.activation(
                        gs[:, tq * GRP:(tq + 1) * GRP], ps[:], AF.Silu), inc=sACT)
                    if "gs0" in dumps and half == 0 and grp == 0 and tq == 0:
                        p.wait("sync", sACT, silu_done[grp][tq])
                        p.run("sync", lambda e: e.dma_start(out=dump_aps["gs0"], in_=gsilu[:, 0:GRP]), inc=sQ, k=16)
                gmm_done[grp] = sPE.n
                if grp > 0:
                    p.wait("gpsimd", sPE, umm_done[grp - 1])
                elif half == 0:
                    p.wait("gpsimd", sACT, wufree)
                wul = p.run("gpsimd", lambda e, grp=grp: e.dma_start(
                    out=wusb_v[:, 0:8, :], in_=wu_r[grp][:, 0:8, :]), inc=sWUp[0], k=16)
                wuh = p.run("gpsimd", lambda e, grp=grp: e.dma_start(
                    out=wusb_v[:, 8:16, :], in_=wu_r[grp][:, 8:16, :]), inc=sWUp[1], k=16)
                wu_done[grp] = (wul, wuh)

                Q4 = GRP // 4
                sq_of = sq_done2[grp]
                cmp_of = cmp_done2[grp]

                def sparsify(tq):
                    s_buf = hsq2[tq % 2]
                    hb_buf = hbuf2[tq % 2]
                    p.wait("vector", sACT, sq_of[tq])
                    s2a = s_buf[:].rearrange("p (gh two) -> p gh two", two=2)
                    lob = mmx[:, 0:2 * Q4]
                    hib = mmx[:, 2 * Q4:4 * Q4]
                    mx = mmx[:, 4 * Q4:5 * Q4]
                    mn = mmx[:, 5 * Q4:6 * Q4]
                    tt = mmx[:, 6 * Q4:7 * Q4]
                    mask = mmx[:, 0:4 * Q4]
                    p.run("vector", lambda e, s2a=s2a, lob=lob: e.tensor_tensor(
                        lob, s2a[:, :, 0], s2a[:, :, 1], ALU.min))
                    v2 = p.run("vector", lambda e, s2a=s2a, hib=hib: e.tensor_tensor(
                        hib, s2a[:, :, 0], s2a[:, :, 1], ALU.max), inc=sVS)
                    p.wait("vector", sVS, v2)
                    lob2 = lob.rearrange("p (g two) -> p g two", two=2)
                    hib2 = hib.rearrange("p (g two) -> p g two", two=2)
                    p.run("vector", lambda e, lob2=lob2, mx=mx: e.tensor_tensor(
                        mx, lob2[:, :, 0], lob2[:, :, 1], ALU.max))
                    v3 = p.run("vector", lambda e, hib2=hib2, mn=mn: e.tensor_tensor(
                        mn, hib2[:, :, 0], hib2[:, :, 1], ALU.min), inc=sVS)
                    p.wait("vector", sVS, v3)
                    v4 = p.run("vector", lambda e, mx=mx, mn=mn, tt=tt: e.tensor_tensor(
                        tt, mx, mn, ALU.min), inc=sVS)
                    p.wait("vector", sVS, v4)
                    sview = s_buf[:].rearrange("p (g four) -> p g four", four=4)
                    mview = mask.rearrange("p (g four) -> p g four", four=4)
                    for j in range(4):
                        p.run("vector", lambda e, sview=sview, mview=mview, j=j, tt=tt: e.tensor_tensor(
                            mview[:, :, j], sview[:, :, j], tt, ALU.is_gt),
                            inc=sVS if j == 3 else None)
                    v8 = sVS.n
                    cmp_of[tq] = v8
                    p.wait("vector", sVS, v8)
                    if tq >= 1:
                        p.wait("vector", sHTg[grp], hs_tr[grp][tq - 1])
                    elif grp > 0:
                        p.wait("vector", sHTg[grp - 1], hs_tr[grp - 1][7])
                    elif half == 1:
                        p.wait("vector", sHTg[NGRP - 1], prev_half_hstr)
                    ha = p.run("vector", lambda e, hb_buf=hb_buf, mask=mask: e.tensor_mul(
                        hs_sb[:], hb_buf[:], mask), inc=sDVE)

                    ic0 = grp * 4
                    tens = T1 if ic0 < 32 else T2
                    oview = tens[:].rearrange("p (ic t) -> p ic t", ic=32)[
                        :, (ic0 % 32):(ic0 % 32) + 4, tq * P:(tq + 1) * P]
                    if half == 1 and grp == 0 and tq == 0:
                        p.wait("sync", sDVE, ln2_last_v)
                    p.wait("sync", sDVE, ha)
                    hs_tr[grp][tq] = p.run("sync", lambda e, oview=oview: e.dma_start(
                        out=oview, in_=hs_sb[:], transpose=True), inc=sHTg[grp], k=16)

                for tq in range(8):
                    ps = ps_g4[tq % 4]
                    if tq >= 4:
                        p.wait("tensor", sDVE, hb_done[grp][tq - 4])
                    else:
                        p.wait("tensor", sACT, silu_done[grp][4 + tq])
                    for kc in range(KC):
                        if tq == 0 and kc in (0, 8):
                            p.wait("tensor", sWUp[kc // 8], wu_done[grp][kc // 8])
                        p.run("tensor", lambda e, kc=kc, tq=tq, ps=ps: e.matmul(
                            ps[:], xap(kc, tq), wu_ap(kc),
                            start=(kc == 0), stop=(kc == KC - 1)),
                            inc=sPE if kc == KC - 1 else None)
                    ud = sPE.n
                    p.wait("vector", sPE, ud)
                    if tq >= 2:
                        p.wait("vector", sACT, sq_of[tq - 2])
                    elif grp > 0:
                        p.wait("vector", sACT, sq_done2[grp - 1][tq + 6])
                    elif half == 1 and tq == 0:
                        p.wait("vector", sACT, prev_half_lnsq)
                    hb = p.run("vector", lambda e, tq=tq, ps=ps, gs=gs_ap(grp): e.tensor_mul(
                        hbuf2[tq % 2][:], gs[:, tq * GRP:(tq + 1) * GRP], ps[:]), inc=sDVE)
                    hb_done[grp][tq] = hb
                    p.wait("scalar", sDVE, hb)
                    if tq >= 2:
                        p.wait("scalar", sVS, cmp_of[tq - 2])
                    elif grp > 0:
                        p.wait("scalar", sVS, cmp_done2[grp - 1][tq + 6])
                    sq_of[tq] = p.run("scalar", lambda e, tq=tq: e.activation(
                        hsq2[tq % 2][:], hbuf2[tq % 2][:], AF.Square), inc=sACT)
                    if "hb0" in dumps and half == 0 and grp == 0 and tq == 0:
                        p.wait("sync", sDVE, hb)
                        p.run("sync", lambda e: e.dma_start(out=dump_aps["hb0"], in_=hbuf2[0][:]), inc=sQ, k=16)
                    if tq >= 1:
                        sparsify(tq - 1)
                sparsify(7)
                umm_done[grp] = sPE.n
                if grp + 1 < NGRP:
                    p.wait("gpsimd", sPE, gmm_done[grp])
                    wl = p.run("gpsimd", lambda e, grp=grp: e.dma_start(
                        out=wgsb_v[:, 0:8, :], in_=wg_r[grp + 1][:, 0:8, :]), inc=sWGp[0], k=16)
                    wh = p.run("gpsimd", lambda e, grp=grp: e.dma_start(
                        out=wgsb_v[:, 8:16, :], in_=wg_r[grp + 1][:, 8:16, :]), inc=sWGp[1], k=16)
                    wg_done[grp + 1] = (wl, wh)

            prev_half_hb = hb_done[NGRP - 1][7]
            prev_half_hstr = hs_tr[NGRP - 1][7]

            if "hst0" in dumps and half == 0:
                p.wait("sync", sHTg[NGRP - 1], hs_tr[NGRP - 1][7])
                p.run("sync", lambda e: e.dma_start(out=dump_aps["hst0"], in_=hsT_ap(0)), inc=sQ, k=16)


            wd_done = [None] * KC
            dmm_done = [None] * KC
            devac_done = [None] * KC
            p.wait("gpsimd", sPE, gmm_done[NGRP - 1])
            wd_done[0] = p.run("gpsimd", lambda e: e.dma_start(
                out=wd_slot(0).rearrange("p (ic c) -> p ic c", ic=IC),
                in_=wd_r[0]), inc=sWDp[0], k=16)
            p.wait("gpsimd", sPE, umm_done[NGRP - 1])
            wd_done[1] = p.run("gpsimd", lambda e: e.dma_start(
                out=wd_slot(1).rearrange("p (ic c) -> p ic c", ic=IC),
                in_=wd_r[1]), inc=sWDp[1], k=16)

            for hc in range(KC):
                ps = ps_d[hc % 2]
                p.wait("tensor", sWDp[hc % 2], wd_done[hc])
                if hc >= 2:
                    p.wait("tensor", sACT, devac_done[hc - 2])
                for ic in range(IC):
                    if hc == 0 and ic % 4 == 0:
                        p.wait("tensor", sHTg[ic // 4], hs_tr[ic // 4][7])
                    for n in range(2):
                        p.run("tensor", lambda e, ic=ic, n=n, hc=hc, ps=ps: e.matmul(
                            ps[:, n * 512:(n + 1) * 512],
                            wd_slot(hc)[:, ic * P:(ic + 1) * P],
                            hsT_ap(ic, n * 512, 512),
                            start=(ic == 0), stop=(ic == IC - 1)),
                            inc=sPE if (ic == IC - 1 and n == 1) else None)
                dmm_done[hc] = sPE.n
                p.wait("scalar", sPE, dmm_done[hc])
                devac_done[hc] = p.run("scalar", lambda e, hc=hc, ps=ps: e.copy(
                    downT_ap(hc), ps[:]), inc=sACT)
                if hc + 2 < KC:
                    p.wait("gpsimd", sPE, dmm_done[hc])
                    wd_done[hc + 2] = p.run("gpsimd", lambda e, hc=hc: e.dma_start(
                        out=wd_slot(hc + 2).rearrange("p (ic c) -> p ic c", ic=IC),
                        in_=wd_r[hc + 2]), inc=sWDp[hc % 2], k=16)
            prev_half_dmm = dmm_done[KC - 1]

            dtm_tr = []
            p.wait("sync", sACT, devac_done[-1])
            for hc in range(KC):
                oview = T1[:, 0:8 * H].rearrange("p (tq h) -> p tq h", tq=8)[
                    :, :, hc * P:(hc + 1) * P]
                v = p.run("sync", lambda e, hc=hc, oview=oview: e.dma_start(
                    out=oview, in_=downT_ap(hc), transpose=True), inc=sQ, k=16)
                dtm_tr.append(v)
            if half == 0:
                p.wait("scalar", sXSP, x1t_spill)
                for hc in range(KC):
                    p.wait("scalar", sQ, dtm_tr[hc])
                    next_x1t_load = p.run("scalar", lambda e, hc=hc: e.dma_start(
                        out=T3x_v[:, hc:hc + 1, :],
                        in_=scr_x1t_v[:, hc:hc + 1, HALF:2 * HALF]), inc=sXL2, k=16)
            if "dtm0" in dumps and half == 0:
                p.wait("sync", sQ, dtm_tr[-1])
                p.run("sync", lambda e: e.dma_start(out=dump_aps["dtm0"], in_=dtm_ap(0)), inc=sQ, k=16)

            if half == 0:
                p.wait("gpsimd", sPE, dmm_done[KC - 1])
                wl = p.run("gpsimd", lambda e: e.dma_start(
                    out=wgsb_v[:, 0:8, :], in_=wg_r[0][:, 0:8, :]), inc=sWGp[0], k=16)
                wh = p.run("gpsimd", lambda e: e.dma_start(
                    out=wgsb_v[:, 8:16, :], in_=wg_r[0][:, 8:16, :]), inc=sWGp[1], k=16)
                next_wg0 = (wl, wh)

            osb = SCR[:, 0:4096].bitcast(F32)
            sdump = SCR[:, 4096:6144]

            r1_load = [None] * 8
            ost = [None] * 8
            rv_of = [None] * 8
            sq2_of = [None] * 8
            p.wait("sync", sPE, prev_half_dmm)
            for i in range(2):
                r1_load[i] = p.run("sync", lambda e, i=i, half=half: e.dma_start(
                    out=r1l2[i], in_=scr_rows[half * 8 + i]), inc=sRL[i % 2], k=16)

            def ln2_stats(tql):
                tqc = half * 8 + tql
                r2 = r2l2[tql % 2]
                b = (tql % 2) * 8
                p.wait("scalar", sDVE, rv_of[tql])
                cp = p.run("scalar", lambda e, r2=r2, b=b: e.activation(
                    sdump[:], r2, AF.Copy, accum_out=lncol[:, b:b + 1]), inc=sACT)
                sq = p.run("scalar", lambda e, r2=r2, b=b: e.activation(
                    sdump[:], r2, AF.Square, accum_out=lncol[:, b + 1:b + 2]), inc=sACT)
                sq2_of[tql] = sq
                p.wait("vector", sACT, cp)
                mv = p.run("vector", lambda e, b=b: e.tensor_scalar(
                    lncol[:, b + 5:b + 6], lncol[:, b:b + 1], 1.0 / H, None, ALU.mult), inc=sVS)
                p.wait("vector", sVS, mv)
                m2 = p.run("vector", lambda e, b=b: e.tensor_scalar(
                    lncol[:, b + 2:b + 3], lncol[:, b + 5:b + 6], lncol[:, b + 5:b + 6],
                    -EPS, ALU.mult, ALU.add), inc=sVS)
                p.wait("vector", sACT, sq)
                p.wait("vector", sVS, m2)
                vpe = p.run("vector", lambda e, b=b: e.tensor_scalar(
                    lncol[:, b + 3:b + 4], lncol[:, b + 1:b + 2], 1.0 / H,
                    lncol[:, b + 2:b + 3], ALU.mult, ALU.subtract), inc=sVS)
                p.wait("scalar", sVS, vpe)
                lnv = p.run("scalar", lambda e, b=b: e.activation(
                    lncol[:, b + 4:b + 5], lncol[:, b + 3:b + 4], AF.Ln), inc=sSS)
                p.wait("scalar", sSS, lnv)
                rs = p.run("scalar", lambda e, b=b: e.activation(
                    lncol[:, b + 6:b + 7], lncol[:, b + 4:b + 5], AF.Exp, scale=-0.5), inc=sACT)
                p.wait("vector", sACT, rs)
                if tql >= 1:
                    p.wait("vector", sQ, ost[tql - 1])
                ov = p.run("vector", lambda e, r2=r2, b=b: e.tensor_scalar(
                    osb[:], r2, lncol[:, b + 5:b + 6], lncol[:, b + 6:b + 7],
                    ALU.subtract, ALU.mult), inc=sDVE)
                p.wait("sync", sDVE, ov)
                ost[tql] = p.run("sync", lambda e, tqc=tqc: e.dma_start(
                    out=out_rows[tqc], in_=osb[:]), inc=sQ, k=16)
                return ov

            for tql in range(8):
                tqc = half * 8 + tql
                p.wait("vector", sRL[tql % 2], r1_load[tql])
                if tql == 0:
                    p.wait("vector", sQ, dtm_tr[-1])
                if tql >= 2:
                    p.wait("vector", sACT, sq2_of[tql - 2])
                rv = p.run("vector", lambda e, tql=tql, tqc=tqc: e.scalar_tensor_tensor(
                    r2l2[tql % 2], r1l2[tql % 2], rcols[:, tqc:tqc + 1], dtm_ap(tql),
                    ALU.mult, ALU.add), inc=sDVE)
                rv_of[tql] = rv
                if tql + 2 < 8:
                    p.wait("sync", sDVE, rv)
                    r1_load[tql + 2] = p.run("sync", lambda e, tql=tql, half=half: e.dma_start(
                        out=r1l2[tql % 2], in_=scr_rows[half * 8 + tql + 2]), inc=sRL[tql % 2], k=16)
                if tql >= 1:
                    ov = ln2_stats(tql - 1)
            ov = ln2_stats(7)
            ln2_last_v = ov
            ln2_last_store = ost[7]
            prev_half_lnsq = sq2_of[7]


        e_psum.close()

        for s in (sQ, *sHTg, sPC, sXSP, sXL2, sRL[0], sRL[1]):
            p.wait("sync", s, s.n)
        for s in (sG, sWV, *sXC, *sWB, *sXL, sSP, *sWGp, *sWUp, *sWDp):
            p.wait("gpsimd", s, s.n)

        block = top.enter_context(nc.Block())
        p.replay(block)

    return nc



def _make_tables():
    i = np.arange(32, dtype=np.float64)
    freq = ROPE_THETA ** (-i / 32.0)
    t = np.arange(T, dtype=np.float64)
    ang = np.outer(freq, t)
    c = np.cos(ang)
    s = np.sin(ang)
    ctab = np.ones((P, T), np.float32)
    stab = np.zeros((P, T), np.float32)
    for ii in range(32):
        ctab[2 * ii] = c[ii]
        ctab[2 * ii + 1] = c[ii]
        stab[2 * ii] = -s[ii]
        stab[2 * ii + 1] = s[ii]
    permm = np.zeros((P, P), ml_dtypes.bfloat16)
    for a in range(0, P, 2):
        permm[a, a + 1] = 1
        permm[a + 1, a] = 1
    onesm = np.ones((P, P), ml_dtypes.bfloat16)
    identm = np.eye(P).astype(ml_dtypes.bfloat16)
    return ctab.astype(ml_dtypes.bfloat16), stab.astype(ml_dtypes.bfloat16), permm, onesm, identm


def _prep_weights(Wq, Wk, Wg, Wu, Wd):
    wq_r = np.ascontiguousarray(Wq.reshape(KC, P, NH, DH).transpose(2, 1, 0, 3))
    wk_r = np.ascontiguousarray(Wk.reshape(KC, P, NKV, DH).transpose(2, 1, 0, 3))
    wg_r = np.ascontiguousarray(Wg.reshape(KC, P, NGRP, GRP).transpose(2, 1, 0, 3))
    wu_r = np.ascontiguousarray(Wu.reshape(KC, P, NGRP, GRP).transpose(2, 1, 0, 3))
    wd_r = np.ascontiguousarray(Wd.reshape(IC, P, KC, P).transpose(2, 1, 0, 3))
    return wq_r, wk_r, wg_r, wu_r, wd_r


def kernel(L0_attn_hidden_states, L0_attn_Wq, L0_attn_Wk, L0_attn_Wv, L0_attn_Wo,
           L0_ffn_w_gate, L0_ffn_w_up, L0_ffn_w_down):
    hs = np.asarray(L0_attn_hidden_states, np.float32)
    Wq = np.asarray(L0_attn_Wq, np.float32)
    Wk = np.asarray(L0_attn_Wk, np.float32)
    Wv = np.asarray(L0_attn_Wv, np.float32)
    Wo = np.asarray(L0_attn_Wo, np.float32)
    Wg = np.asarray(L0_ffn_w_gate, np.float32)
    Wu = np.asarray(L0_ffn_w_up, np.float32)
    Wd = np.asarray(L0_ffn_w_down, np.float32)

    ctab, stab, permm, onesm, identm = _make_tables()
    wq_r, wk_r, wg_r, wu_r, wd_r = _prep_weights(Wq, Wk, Wg, Wu, Wd)

    nc = build_core_program()
    base = {
        "wq_r": wq_r, "wk_r": wk_r, "wv": np.ascontiguousarray(Wv),
        "wo": np.ascontiguousarray(Wo),
        "wg_r": wg_r, "wu_r": wu_r, "wd_r": wd_r,
        "ctab": ctab, "stab": stab, "perm": permm, "onesm": onesm, "ident": identm,
    }
    in_maps = []
    for core in range(8):
        m = dict(base)
        m["x"] = np.ascontiguousarray(hs[core])
        in_maps.append(m)
    res = run_bass_kernel_spmd(nc, in_maps, core_ids=list(range(8)))
    out = np.stack([r["out"] for r in res.results], axis=0)
    return out.astype(np.float32)
